# version 13
# baseline (speedup 1.0000x reference)
import numpy as np
from contextlib import ExitStack

import concourse.bass as bass
import concourse.mybir as mybir
from concourse.bass_utils import run_bass_kernel_spmd

F32 = mybir.dt.float32
BF16 = mybir.dt.bfloat16
AF = mybir.ActivationFunctionType
ALU = mybir.AluOpType

S = 2048
BC = 4
NCH = 8
NT5 = 4
NT1 = 16
C_A, C_F, C_B = 12.0, 14.0, 8.0
EPS_P, EPS_H, EPS_2 = 1e-5, 1e-5, 1e-6
NSUB = 4
DBG = None
DBG_NJT = None
DBG_QS = None
DBG_BUF = "abuf"

ENGS = ("sync", "gpsimd", "tensor", "scalar", "vector")


class Prog:
    def __init__(self, nc, stack):
        self.nc = nc
        self.stack = stack
        self.steps = {e: [] for e in ENGS}
        self.sems = {}
        self.tot = {}
        self.nbar = 0

    def sem(self, name):
        if name not in self.sems:
            self.sems[name] = self.stack.enter_context(self.nc.semaphore(name))
            self.tot[name] = 0
        return name

    def op(self, eng, fn, waits=(), inc=None, incby=1):
        if inc is None:
            incs = ()
        elif isinstance(inc, str):
            incs = ((inc, incby),)
        else:
            incs = tuple(inc)
        post = None
        for s, by in incs:
            self.sem(s)
            self.tot[s] += by
            if post is None:
                post = self.tot[s]
        for s, _ in waits:
            self.sem(s)
        sems = self.sems
        w = tuple((sems[s], int(v)) for s, v in waits if v > 0)

        def thunk(e, fn=fn, w=w, incs=incs):
            for s, v in w:
                e.wait_ge(s, v)
            r = fn(e)
            if incs:
                r.then_inc(sems[incs[0][0]], incs[0][1])
                for s, by in incs[1:]:
                    try:
                        e.drain().then_inc(sems[s], by)
                    except Exception:
                        e.nop().then_inc(sems[s], by)
        self.steps[eng].append(thunk)
        return post

    def wait(self, eng, waits):
        for s, _ in waits:
            self.sem(s)
        waits = [(s, v) for s, v in waits]
        sems = self.sems
        w = tuple((sems[s], int(v)) for s, v in waits if v > 0)

        def thunk(e, w=w):
            for s, v in w:
                e.wait_ge(s, v)
        self.steps[eng].append(thunk)

    def barrier(self):
        ph = self.sem("phase")
        self.nbar += 1
        tgt = 5 * self.nbar
        sems = self.sems
        for e in ENGS:
            def thunk(en, sems=sems, ph=ph):
                try:
                    en.drain()
                except Exception:
                    pass
                en.sem_inc(sems[ph], 1)
            self.steps[e].append(thunk)
        self.tot[ph] = tgt
        for e in ENGS:
            self.wait(e, [(ph, tgt)])

    def finalize(self, block):
        for e in ENGS:
            def runner(eng, e=e):
                for f in self.steps[e]:
                    f(eng)
            getattr(block, e)(runner)


class Res:
    _n = 0

    def __init__(self, P):
        Res._n += 1
        self.P = P
        self.s = P.sem("r%d" % Res._n)
        self.val = 0

    def wfree(self):
        return (self.s, self.val)

    def release(self):
        self.val = self.P.tot[self.s]


def _host_prepare(inp):
    f = np.float32
    c = np.ascontiguousarray
    w = {}
    qkv = np.asarray(inp["L0_attn_a_qkv_w"], f)
    wq = qkv[0:1024].T
    wk = qkv[1024:2048].T
    w["wAqk"] = c(np.concatenate([wq, wk], 1).reshape(8, 128, 2048)
                  .reshape(8, 128, 8, 256).transpose(2, 0, 1, 3))
    w["wAv"] = c(qkv[2048:3072].T.reshape(8, 128, 1024)
                 .reshape(8, 128, 2, 512).transpose(2, 0, 1, 3))
    w["wAproj"] = c(np.asarray(inp["L0_attn_a_proj_w"], f).T.reshape(8, 128, 1024))
    w["sinks"] = c(np.asarray(inp["L0_attn_a_sinks"], f).reshape(1, 16))
    w["wFq"] = c(np.asarray(inp["L0_ffn_Wq"], f).reshape(8, 128, 512))
    w["wFkv"] = c(np.asarray(inp["L0_ffn_Wkv"], f).reshape(8, 128, 1024))
    w["wFs"] = c(np.asarray(inp["L0_ffn_Ws"], f).reshape(8, 128, 8))
    w["wFo"] = c(np.asarray(inp["L0_ffn_Wo"], f).reshape(512, 1024).reshape(4, 128, 1024))
    w["boF"] = c(np.asarray(inp["L0_ffn_bo"], f))
    w["wBq"] = c(np.asarray(inp["L0_attn_b_Wq"], f).reshape(1024, 1024)
                 .reshape(8, 128, 4, 256).transpose(2, 0, 1, 3))
    w["wBk"] = c(np.asarray(inp["L0_attn_b_Wk"], f).reshape(1024, 1024).reshape(8, 128, 1024))
    w["wBv"] = c(np.asarray(inp["L0_attn_b_Wv"], f).reshape(1024, 1024).reshape(8, 128, 1024))
    whq = np.asarray(inp["L0_attn_b_Whq"], f)
    whk = np.asarray(inp["L0_attn_b_Whk"], f)
    w["wBhq"] = c(np.stack([np.concatenate([whq[2 * i], whq[2 * i + 1]], 0)
                            for i in range(8)], 1).reshape(128, 256))
    w["wBhk"] = c(np.stack([np.concatenate([whk[2 * i], whk[2 * i + 1]], 0)
                            for i in range(8)], 1).reshape(128, 256))
    w["wBo"] = c(np.asarray(inp["L0_attn_b_Wo"], f).reshape(1024, 1024)
                 .reshape(8, 128, 8, 128).transpose(2, 0, 1, 3))
    w1 = np.asarray(inp["L0_attn_b_W1"], f).reshape(8, 128, 32, 128)
    w["wB1"] = c(w1.transpose(2, 1, 0, 3).reshape(32, 128, 1024))
    w2 = np.asarray(inp["L0_attn_b_W2"], f).reshape(32, 128, 8, 128)
    w["wB2"] = c(w2.transpose(2, 1, 0, 3).reshape(8, 128, 4096))
    pc = np.zeros((128, 28), f)
    pc[:, 0:8] = np.asarray(inp["L0_attn_a_proj_b"], f).reshape(8, 128).T
    pc[:, 8] = np.tile(np.asarray(inp["L0_attn_a_qn_w"], f), 2)
    pc[:, 9] = np.tile(np.asarray(inp["L0_attn_a_qn_b"], f), 2)
    pc[:, 10] = np.tile(np.asarray(inp["L0_attn_a_kn_w"], f), 2)
    pc[:, 11] = np.tile(np.asarray(inp["L0_attn_a_kn_b"], f), 2)
    pc[:, 12:20] = np.asarray(inp["L0_attn_b_b2"], f).reshape(8, 128).T
    w["pcol"] = c(pc)
    gc = np.zeros((128, 32), f)
    gc[:, 0:8] = np.asarray(inp["L0_attn_b_ln1_g"], f).reshape(8, 128).T
    gc[:, 8:16] = np.asarray(inp["L0_attn_b_ln1_b"], f).reshape(8, 128).T
    gc[:, 16:24] = np.asarray(inp["L0_attn_b_ln2_g"], f).reshape(8, 128).T
    gc[:, 24:32] = np.asarray(inp["L0_attn_b_ln2_b"], f).reshape(8, 128).T
    w["gcol"] = c(gc)
    w["b1"] = c(np.asarray(inp["L0_attn_b_b1"], f).reshape(32, 128).T)

    pos = np.arange(S, dtype=np.float64)
    invf = 1.0 / (10000.0 ** (np.arange(0, 64, 2, dtype=np.float64) / 64.0))
    fr = pos[:, None] * invf[None, :]
    emb = np.concatenate([fr, fr], 1)
    cos64 = np.cos(emb).T.astype(f)
    sin64 = np.sin(emb).T.astype(f)
    w["cos128"] = c(np.concatenate([cos64, cos64], 0))
    w["sin128"] = c(np.concatenate([sin64, sin64], 0))
    kp = np.arange(128)[:, None]
    qf = np.arange(128)[None, :]
    rotP = np.zeros((128, 128), f)
    for m in range(128):
        if m % 64 < 32:
            rotP[m + 32, m] = -1.0
        else:
            rotP[m - 32, m] = 1.0
    w["rotP"] = rotP
    w["maskU"] = c((kp > qf).astype(f))
    w["maskL"] = c((kp <= qf).astype(f))
    w["ones128"] = c(np.ones((128, 1), f))
    w["onesrow"] = c(np.ones((1, 128), f))
    w["identb"] = c(np.eye(128, dtype=f))
    e16 = np.zeros((8, 128, 16), f)
    for ch in range(8):
        e16[ch, 0:64, 2 * ch] = 1.0
        e16[ch, 64:128, 2 * ch + 1] = 1.0
    w["e16"] = c(e16)
    eb1 = np.zeros((8, 16, 128), f)
    for ch in range(8):
        eb1[ch, 2 * ch, 0:64] = 1.0
        eb1[ch, 2 * ch + 1, 64:128] = 1.0
    w["eb1"] = c(eb1)
    eb2 = np.zeros((33, 128), f)
    eb2[0, 0:64] = 1.0
    eb2[32, 64:128] = 1.0
    w["eb2"] = c(eb2)
    e4 = np.zeros((128, 4), f)
    e4b = np.zeros((4, 128), f)
    for i in range(4):
        e4[32 * i:32 * i + 32, i] = 1.0
        e4b[i, 32 * i:32 * i + 32] = 1.0
    w["e4"] = c(e4)
    w["e4b"] = c(e4b)
    import ml_dtypes
    for n in BF_W:
        w[n] = np.ascontiguousarray(
            w[n].astype(ml_dtypes.bfloat16)).view(np.float32)
    return w


BF_W = {"wAproj", "wFq", "wFkv", "wFo", "wBk", "wBv", "wAqk", "wAv",
        "wBq", "wBo", "wB1", "wB2", "wBhq", "wBhk", "cos128", "sin128"}

WSPEC = [
    ("wAqk", (8, 8, 128, 256)), ("wAv", (2, 8, 128, 512)),
    ("wAproj", (8, 128, 1024)),
    ("sinks", (1, 16)), ("wFq", (8, 128, 512)), ("wFkv", (8, 128, 1024)),
    ("wFs", (8, 128, 8)), ("wFo", (4, 128, 1024)), ("boF", (8, 1024)),
    ("wBq", (4, 8, 128, 256)), ("wBk", (8, 128, 1024)), ("wBv", (8, 128, 1024)),
    ("wBhq", (128, 256)), ("wBhk", (128, 256)), ("wBo", (8, 8, 128, 128)),
    ("wB1", (32, 128, 1024)), ("wB2", (8, 128, 4096)),
    ("pcol", (128, 28)), ("gcol", (128, 32)), ("b1", (128, 32)),
    ("cos128", (128, S)), ("sin128", (128, S)),
    ("maskU", (128, 128)), ("maskL", (128, 128)), ("rotP", (128, 128)),
    ("ones128", (128, 1)), ("onesrow", (1, 128)), ("identb", (128, 128)),
    ("e16", (8, 128, 16)), ("eb1", (8, 16, 128)), ("e4", (128, 4)), ("e4b", (4, 128)), ("eb2", (33, 128)),
]


class Kern:
    def __init__(self, nc, stack, nsub):
        self.nc = nc
        self.nsub = nsub
        self.P = Prog(nc, stack)
        self.E = stack.enter_context
        self.dram = {}
        self.T = {}
        self.dram["xfeat"] = nc.dram_tensor("xfeat", [BC, 8, 128, S], F32,
                                            kind="ExternalInput").ap()
        for name, shape in WSPEC:
            sh = list(shape)
            if name in BF_W:
                sh[-1] //= 2
            self.dram[name] = nc.dram_tensor(name, sh, F32,
                                             kind="ExternalInput").ap()
        self.yout = nc.dram_tensor("yout", [BC, 8, 128, S], F32,
                                   kind="ExternalOutput").ap()
        self._alloc()
        P = self.P
        self.rStat = [Res(P), Res(P)]
        self.rGem = [Res(P), Res(P)]
        self.rV = [Res(P), Res(P)]
        self.rYb = [Res(P), Res(P)]
        self.rQS = [Res(P), Res(P)]
        self.rQB = [Res(P), Res(P)]
        self.rSc = [Res(P), Res(P)]
        self.rOT = Res(P)
        self._epi_pend = None
        self.rBc = Res(P)
        self.rGt = Res(P)
        self.rPr = [Res(P), Res(P)]
        self.rYt = Res(P)
        self.rQt = Res(P)
        self.rOp = Res(P)
        self.rW = [Res(P), Res(P), Res(P), Res(P)]
        self.rOut = [Res(P), Res(P)]
        self.rXq = Res(P)
        self.rT12 = [Res(P), Res(P)]
        self.rRowA = Res(P)
        self.rQbb = Res(P)
        self.rKb = Res(P)

    def _alloc(self):
        nc, E, T = self.nc, self.E, self.T

        def sb(name, shape, dt):
            T[name] = E(nc.sbuf_tensor("s_" + name, list(shape), dt))

        sb("stream", (128, 8 * S), BF16)
        sb("abuf", (128, 8 * S), BF16)
        sb("bbuf", (128, 8 * S), BF16)
        sb("vbuf", (128, NT1 * 16 * 65), BF16)
        sb("kbbuf", (128, 4 * S), BF16)
        sb("wslab", (128, 8192), BF16)
        sb("probs", (128, 2048), BF16)
        sb("ybuf", (128, 8 * 512), BF16)
        sb("qbuf", (128, 8 * 512), BF16)
        sb("outp", (128, 8 * 512), BF16)
        sb("qbbuf", (128, 4 * 512), BF16)
        sb("expkb", (128, 512), BF16)
        sb("e4bb", (4, 128), BF16)
        sb("maskU", (128, 128), BF16)
        sb("maskL", (128, 128), BF16)
        sb("identb", (128, 128), BF16)
        sb("e16b", (128, 8 * 16), BF16)
        sb("e4b16", (128, 4), BF16)
        sb("t1b", (128, 512), BF16)
        sb("t2b", (128, 512), BF16)
        sb("Rrow", (97, 512), BF16)
        sb("MRrow", (97, 512), BF16)
        sb("Arow", (97, 512), BF16)
        sb("MArow", (97, 512), BF16)
        sb("hRb", (48, 512), BF16)
        sb("rowsA", (97, 512), BF16)
        sb("rowsB", (97, 512), BF16)
        sb("rowsC", (128, 512), F32)
        sb("gtmp", (128, 32), BF16)
        sb("gm8", (128, 8), F32)
        sb("gs", (128, 8), F32)
        sb("pcol", (128, 28), F32)
        sb("gcol", (128, 32), F32)
        sb("b1col", (128, 32), F32)
        sb("sinkrow", (1, 16), F32)
        sb("ones1b", (128, 1), BF16)
        sb("cbias", (128, 8), F32)
        sb("onesrf", (97, 64), F32)
        sb("eb2f", (33, 128), F32)
        sb("onesrb", (97, 128), BF16)
        class _View:
            def __init__(self, t, p0, p1, c0=None, c1=None):
                self.t, self.p0, self.p1 = t, p0, p1
                self.c0, self.c1 = c0, c1

            def ap(self):
                a = self.t.ap()[self.p0:self.p1, :]
                if self.c0 is not None:
                    a = a[:, self.c0:self.c1]
                return a
        class _ViewC:
            def __init__(self, t, p0, p1, c0, c1, dt=None):
                self.t, self.p0, self.p1 = t, p0, p1
                self.c0, self.c1, self.dt = c0, c1, dt

            def ap(self):
                a = self.t.ap()[self.p0:self.p1, self.c0:self.c1]
                if self.dt is not None:
                    a = a.bitcast(self.dt)
                return a
        T["outsb"] = _ViewC(T["ybuf"], 0, 128, 1536, 2560, F32)
        T["gtr"] = _ViewC(T["ybuf"], 0, 4, 2560, 2688)
        T["boFb"] = _ViewC(T["ybuf"], 0, 8, 2816, 3840)
        T["eb1b"] = _ViewC(T["ybuf"], 0, 48, 2048 + 1024, 2048 + 2048)
        T["zt"] = _View(T["ybuf"], 0, 128, 0, 512)
        T["ropA"] = _View(T["ybuf"], 0, 128, 512, 1024)
        T["xsq"] = _View(T["ybuf"], 0, 128, 1024, 1536)
        T["hSm"] = _View(T["rowsA"], 0, 16)
        T["hR"] = _View(T["rowsB"], 0, 16)
        T["R2row"] = T["Arow"]
        T["MR2row"] = T["MArow"]
        self.psA = E(nc.psum_tensor("psA", [128, 2048], F32))
        self.psB = E(nc.psum_tensor("psB", [128, 2048], F32))

    def A(self, b, n=1):
        return self.psA.ap()[:, b * 512:(b + n) * 512]

    def Bk(self, b, n=1):
        return self.psB.ap()[:, b * 512:(b + n) * 512]

    def sch(self, c, t0, n):
        return self.T["stream"].ap()[:, c * S + t0:c * S + t0 + n]

    def ach(self, c, t0, n):
        return self.T["abuf"].ap()[:, c * S + t0:c * S + t0 + n]

    def bch(self, c, t0, n):
        return self.T["bbuf"].ap()[:, c * S + t0:c * S + t0 + n]

    def kch(self, c, t0, n):
        return self.T["kbbuf"].ap()[:, c * S + t0:c * S + t0 + n]

    def wsl(self, off, ncols):
        return self.T["wslab"].ap()[:, off:off + ncols]

    def dma_g(self, dstfn, srcfn, inc="LD"):
        return self.P.op("gpsimd", lambda e: e.dma_start(out=dstfn(), in_=srcfn()),
                         inc=inc, incby=16)

    def dma_s(self, dstfn, srcfn, inc="WL"):
        return self.P.op("sync", lambda e: e.dma_start(out=dstfn(), in_=srcfn()),
                         inc=inc, incby=16)

    def wdram(self, name):
        sh = dict(WSPEC)[name]
        if len(sh) == 3:
            return self.dram[name]
        return self.dram[name]

    def load_w8(self, name, off, cols, nch=8):
        bw = name in BF_W
        for c in range(nch):
            src = self.wdram(name)[c]
            self.dma_g(lambda c=c, off=off, cols=cols, bw=bw:
                       self.wsl(off + c * cols, cols).bitcast(F32)
                       if bw else self.wsl(off + c * cols, cols),
                       lambda src=src: src)
        return self.P.tot["LD"]

    def load_w8_sliced(self, name, off, cols, csl, res):
        self.P.wait("gpsimd", [res.wfree()])
        idx = csl[0] // cols
        for c in range(NCH):
            src = self.dram[name][idx, c]
            self.dma_g(lambda c=c, off=off, cols=cols:
                       self.wsl(off + c * cols, cols).bitcast(F32),
                       lambda src=src: src)
        return self.P.tot["LD"]

    def load_w1(self, name, off, m, res):
        self.P.wait("gpsimd", [res.wfree()])
        bw = name in BF_W
        sh = dict(WSPEC)[name]
        src = self.dram[name][m]
        self.dma_g(lambda off=off, n=sh[2], bw=bw:
                   self.wsl(off, n).bitcast(F32) if bw else self.wsl(off, n),
                   lambda src=src: src)
        return self.P.tot["LD"]

    def prologue(self):
        P, T, dram = self.P, self.T, self.dram
        ds = self.dma_s
        self.dma_g(lambda: T["maskU"].ap(), lambda: dram["maskU"])
        self.dma_g(lambda: T["maskL"].ap(), lambda: dram["maskL"])
        self.dma_g(lambda: T["identb"].ap(), lambda: dram["identb"])
        P.op("vector", lambda e: e.memset(T["ones1b"].ap(), 1.0), inc="V")
        P.op("vector", lambda e: e.memset(T["onesrf"].ap(), 1.0), inc="V")
        P.op("vector", lambda e: e.memset(T["rowsC"].ap(), 0.0), inc="V")
        ds(lambda: T["eb2f"].ap(), lambda: dram["eb2"])
        P.op("vector", lambda e: e.memset(T["onesrb"].ap(), 1.0), inc="V")
        self.dma_g(lambda: T["e16b"].ap().rearrange("p (c n) -> p c n", n=16),
                   lambda: dram["e16"].transpose([1, 0, 2]))
        self.dma_g(lambda: T["e4b16"].ap(), lambda: dram["e4"])
        self.dma_g(lambda: T["e4bb"].ap(), lambda: dram["e4b"])
        ds(lambda: T["pcol"].ap(), lambda: dram["pcol"])
        ds(lambda: T["gcol"].ap(), lambda: dram["gcol"])
        ds(lambda: T["b1col"].ap(), lambda: dram["b1"])
        ds(lambda: T["sinkrow"].ap(), lambda: dram["sinks"])
        import math as _m
        for i, v in enumerate([-C_A, -C_F, -C_B, 1024.0 * 1024.0 * EPS_P,
                               _m.log(1024.0), 4096.0 * EPS_H, _m.log(64.0),
                               1024.0 * 1024.0 * EPS_2]):
            P.op("vector", lambda e, i=i, v=v: e.memset(
                T["cbias"].ap()[:, i:i + 1], v), inc="V")
        wl = P.tot["WL"]
        P.op("scalar", lambda e: e.activation(T["sinkrow"].ap(), T["sinkrow"].ap(),
                                              AF.Exp, bias=T["cbias"].ap()[0:1, 0:1], scale=1.0),
             waits=[("WL", wl), ("LD", P.tot["LD"]), ("V", P.tot["V"])], inc="C")
        P.barrier()

    def stats_plain(self, Rdst, MRdst, eps, srcfn=None, hold_rows=False):
        P, T = self.P, self.T
        srcfn = srcfn or self.sch
        P.wait("tensor", [self.rStat[0].wfree(), self.rStat[1].wfree(),
                          self.rRowA.wfree()])
        P.wait("vector", [self.rStat[0].wfree(), self.rStat[1].wfree()])
        vms = P.op("vector", lambda e: e.memset(
            self.psA.ap()[:, 0:1024], 0.0), inc="V")
        tgrp = 0
        for j in range(NT5):
            for c in range(NCH):
                w = []
                if j == 0 and c == 0:
                    w.append(("V", vms))
                if j > 0 and c == 0:
                    w.append(("T", tgrp))
                P.op("tensor",
                     lambda e, j=j, c=c: e.matmul(
                         self.psA.ap()[32 * j:32 * j + 1, 0:512],
                         T["ones1b"].ap(), srcfn(c, 512 * j, 512),
                         start=(c == 0), stop=(c == NCH - 1),
                         tile_position=(0, 32 * j)),
                     waits=w,
                     inc="T" if c == NCH - 1 else None)
            tgrp = P.tot["T"]
        for j in range(NT5):
            for c in range(NCH):
                vsq = P.op("vector",
                           lambda e, j=j, c=c: e.tensor_tensor(
                               T["xsq"].ap(), srcfn(c, 512 * j, 512),
                               srcfn(c, 512 * j, 512), ALU.mult),
                           waits=[self.rXq.wfree()], inc="V")
                w = [("V", vsq)]
                if j > 0 and c == 0:
                    w.append(("T", tgrp))
                P.op("tensor",
                     lambda e, j=j, c=c: e.matmul(
                         self.psA.ap()[32 * j:32 * j + 1, 512:1024],
                         T["ones1b"].ap(), T["xsq"].ap(),
                         start=(c == 0), stop=(c == NCH - 1),
                         tile_position=(0, 32 * j)),
                     waits=w,
                     inc=[(self.rXq.s, 1), ("T", 1)] if c == NCH - 1
                     else self.rXq.s)
                self.rXq.release()
            tgrp = P.tot["T"]
        tmm = P.op("tensor", lambda e: e.drain(), inc="T")
        P.op("vector", lambda e: e.tensor_copy(T["rowsB"].ap(),
                                               self.psA.ap()[0:97, 0:512]),
             waits=[("T", tmm)], inc=self.rStat[0].s)
        self.rStat[0].release()
        P.op("vector", lambda e: e.tensor_tensor(T["rowsA"].ap(), T["rowsB"].ap(),
                                                 T["rowsB"].ap(), ALU.mult))
        if DBG == "a_stats":
            P.op("vector", lambda e: e.tensor_copy(
                T["outp"].ap()[0:97, 512:1024], T["rowsA"].ap()))
        P.op("vector", lambda e: e.scalar_tensor_tensor(
            T["rowsA"].ap(), self.psA.ap()[0:97, 512:1024], 1024.0,
            T["rowsA"].ap(), ALU.mult, ALU.subtract), inc=self.rStat[1].s)
        self.rStat[1].release()
        if DBG == "a_stats":
            P.op("vector", lambda e: e.tensor_copy(
                T["outp"].ap()[0:97, 1024:1536], T["rowsA"].ap()))
        vr = P.op("vector", lambda e: e.drain(), inc="V")
        ecol = 3 if eps == EPS_P else 7
        P.op("scalar", lambda e, ecol=ecol: e.activation(
            T["rowsA"].ap(), T["rowsA"].ap(), AF.Ln,
            bias=self.T["cbias"].ap()[0:97, ecol:ecol + 1], scale=1.0),
            waits=[("V", vr)])
        if DBG == "a_stats":
            P.op("scalar", lambda e: e.copy(
                T["outp"].ap()[0:97, 1536:2048], T["rowsA"].ap()))
        cr = P.op("scalar", lambda e: e.activation(
            T["rowsA"].ap(), T["rowsA"].ap(), AF.Exp,
            bias=self.T["cbias"].ap()[0:97, 4:5], scale=-0.5), inc="C")
        P.op("vector", lambda e, Rdst=Rdst: e.tensor_copy(T[Rdst].ap(),
                                                          T["rowsA"].ap()),
             waits=[("C", cr)])
        incs = [("V", 1)] if hold_rows else [("V", 1), (self.rRowA.s, 1)]
        P.op("vector", lambda e, MRdst=MRdst: e.scalar_tensor_tensor(
            T[MRdst].ap(), T["rowsB"].ap(), 1.0 / 1024.0, T["rowsA"].ap(),
            ALU.mult, ALU.mult), inc=incs)
        if not hold_rows:
            self.rRowA.release()
        return P.tot["V"]

    def bcast(self, dst, rname, j, q0, n, waits=(), tpcol=0, incs=None):
        return self.P.op("tensor",
                         lambda e: e.matmul(
                             dst, self.T["onesrb"].ap()[32 * j:32 * j + 1, :],
                             self.T[rname].ap()[32 * j:32 * j + 1, q0:q0 + n],
                             start=True, stop=True,
                             tile_position=(32 * j, tpcol)),
                         waits=waits, inc=incs or "T")

    def ytile(self, j, Rname, MRname, dstfn, vwait=0, srcfn=None):
        P, T = self.P, self.T
        srcfn = srcfn or self.sch
        if DBG == "noln":
            for c in range(NCH):
                P.op("vector",
                     lambda e, c=c: e.tensor_copy(dstfn(c), srcfn(c, 512 * j, 512)),
                     inc="V" if c == NCH - 1 else None)
            return P.tot["V"]
        w = [self.rYb[0].wfree()]
        if vwait:
            w.append(("V", vwait))
        self.bcast(self.Bk(0), Rname, j, 0, 512, waits=w)
        tM = self.bcast(self.Bk(1), MRname, j, 0, 512,
                        waits=[self.rYb[1].wfree()])
        for c in range(NCH):
            sl = c % 2
            tb = "t1b" if sl == 0 else "t2b"
            P.op("vector",
                 lambda e, c=c, tb=tb: e.tensor_tensor(
                     T[tb].ap(), srcfn(c, 512 * j, 512), self.Bk(0), ALU.mult),
                 waits=([("T", tM)] if c == 0 else []) + [self.rT12[sl].wfree()],
                 inc=([(self.rYb[0].s, 1)] if c == NCH - 1 else None))
            P.op("vector",
                 lambda e, c=c, tb=tb: e.tensor_tensor(
                     dstfn(c), T[tb].ap(), self.Bk(1), ALU.subtract),
                 inc=([(self.rYb[1].s, 1), (self.rT12[sl].s, 1), ("V", 1)]
                      if c == NCH - 1 else [(self.rT12[sl].s, 1)]))
            self.rT12[sl].release()
        self.rYb[0].release()
        self.rYb[1].release()
        return P.tot["V"]

    def y1tile(self, j, dstfn):
        P, T = self.P, self.T
        self.bcast(self.Bk(0), "Arow", j, 0, 512, waits=[self.rYb[0].wfree()])
        tM = self.bcast(self.Bk(1), "MArow", j, 0, 512,
                        waits=[self.rYb[1].wfree()])
        for c in range(NCH):
            sl = c % 2
            tb = "t1b" if sl == 0 else "t2b"
            P.op("vector",
                 lambda e, c=c, tb=tb: e.tensor_tensor(
                     T[tb].ap(), self.sch(c, 512 * j, 512), self.Bk(0), ALU.mult),
                 waits=([("T", tM)] if c == 0 else []) + [self.rT12[sl].wfree()])
            vv = P.op("vector",
                      lambda e, c=c, tb=tb: e.tensor_tensor(
                          T["t3b"].ap() if False else T[tb].ap(),
                          T[tb].ap(), self.Bk(1), ALU.subtract),
                      inc=([("V", 1), (self.rYb[0].s, 1), (self.rYb[1].s, 1)]
                           if c == NCH - 1 else "V"))
            P.op("scalar",
                 lambda e, c=c, tb=tb: e.activation(
                     dstfn(c), T[tb].ap(), AF.Identity,
                     bias=T["gcol"].ap()[:, 8 + c:9 + c],
                     scale=T["gcol"].ap()[:, c:c + 1]),
                 waits=[("V", vv)], inc=[("C", 1), (self.rT12[sl].s, 1)])
            self.rT12[sl].release()
        self.rYb[0].release()
        self.rYb[1].release()
        return P.tot["C"]

    def attn_a(self, bi, final):
        P, T = self.P, self.T
        self.stats_plain("Rrow", "MRrow", EPS_P)
        P.barrier()
        if DBG == "a_stats":
            P.op("vector", lambda e: e.tensor_copy(
                T["outp"].ap()[0:97, 0:512], self.psA.ap()[0:97, 512:1024]),
                inc="V")
            P.wait("gpsimd", [("V", P.tot["V"])])
            if bi == 0:
                self.dma_g(lambda: self.yout[0, 4][0:97, 0:512],
                           lambda: T["outp"].ap()[0:97, 0:512], inc="YD")
                self.dma_g(lambda: self.yout[0, 5][0:97, 0:512],
                           lambda: T["outp"].ap()[0:97, 512:1024], inc="YD")
                self.dma_g(lambda: self.yout[0, 6][0:97, 0:512],
                           lambda: T["outp"].ap()[0:97, 1024:1536], inc="YD")
                self.dma_g(lambda: self.yout[0, 7][0:97, 0:512],
                           lambda: T["outp"].ap()[0:97, 1536:2048], inc="YD")
                self.dma_g(lambda: self.yout[0, 0][0:97, 0:512],
                           lambda: T["Rrow"].ap(), inc="YD")
                self.dma_g(lambda: self.yout[0, 1][0:97, 0:512],
                           lambda: T["MRrow"].ap(), inc="YD")
                self.dma_g(lambda: self.yout[0, 2][0:97, 0:512],
                           lambda: T["rowsA"].ap(), inc="YD")
                self.dma_g(lambda: self.yout[0, 3][0:97, 0:512],
                           lambda: T["rowsB"].ap(), inc="YD")
            P.wait("gpsimd", [("YD", P.tot.get("YD", 0))])
            P.barrier()
            return

        if bi == 0:
            P.op("vector", lambda e: e.memset(
                T["vbuf"].ap().rearrange("p (a c) -> p a c", c=65)[:, :, 64:65],
                1.0), inc="V")

        ymap = [("qbuf", 0), ("qbuf", 2048), ("outp", 0), ("outp", 2048),
                ("probs", 0), ("probs", 512), ("probs", 1024), ("probs", 1536)]

        def yfull(c, off=0, n=512):
            tn, base = ymap[c]
            return self.T[tn].ap()[:, base + off: base + off + n]

        wslot = [0, 2048]
        for j in range(NT5):
            yv = self.ytile(j, "Rrow", "MRrow", lambda c: yfull(c))
            for op_ in range(8):
                sl = op_ % 2
                ld = self.load_w8_sliced("wAqk", wslot[sl], 256,
                                         (op_ * 256, op_ * 256 + 256), self.rW[sl])
                for oc2 in range(2):
                    oc = 2 * op_ + oc2
                    bk = oc % 2
                    for c in range(NCH):
                        P.op("tensor",
                             lambda e, c=c, oc2=oc2, sl=sl, bk=bk: e.matmul(
                                 self.A(bk),
                                 self.wsl(wslot[sl] + c * 256 + oc2 * 128, 128),
                                 yfull(c),
                                 start=(c == 0), stop=(c == NCH - 1)),
                             waits=([("LD", ld), self.rGem[bk].wfree()] +
                                    ([("V", yv)] if op_ == 0 and oc2 == 0 else []))
                             if c == 0 else [],
                             inc="T" if c == NCH - 1 else None)
                    tq = P.tot["T"]
                    dstT = "abuf" if oc < 8 else "bbuf"
                    cc = oc % 8
                    incs = [(self.rGem[bk].s, 1)]
                    if oc2 == 1:
                        incs.append((self.rW[sl].s, 1))
                    P.op("scalar",
                         lambda e, dstT=dstT, cc=cc, bk=bk, j=j: e.activation(
                             T[dstT].ap()[:, cc * S + 512 * j:
                                          cc * S + 512 * j + 512],
                             self.A(bk), AF.Copy, bias=0.0, scale=1.0),
                         waits=[("T", tq)], inc=incs)
                    self.rGem[bk].release()
                    if oc2 == 1:
                        self.rW[sl].release()
            ldv0 = self.load_w8_sliced("wAv", 0, 512, (0, 512), self.rW[0])
            ldv1 = self.load_w8_sliced("wAv", 4096, 512, (512, 1024), self.rW[1])
            for it in range(4):
                tt = 4 * j + it
                for half in range(2):
                    bk = 2 + half
                    ldw = ldv0 if half == 0 else ldv1
                    for c in range(NCH):
                        P.op("tensor",
                             lambda e, c=c, it=it, half=half, bk=bk: e.matmul(
                                 self.A(bk), yfull(c, it * 128, 128),
                                 self.wsl(half * 4096 + c * 512, 512),
                                 start=(c == 0), stop=(c == NCH - 1)),
                             waits=[("LD", ldw), self.rV[half].wfree()]
                             if c == 0 else [],
                             inc="T" if c == NCH - 1 else None)
                    tv = P.tot["T"]
                    incs = [(self.rV[half].s, 1)]
                    if it == 3:
                        incs.append((self.rW[half].s, 1))
                    P.op("vector",
                         lambda e, tt=tt, half=half, bk=bk: e.tensor_copy(
                             T["vbuf"].ap().rearrange("p (t h c) -> p t h c",
                                                      t=NT1, h=16, c=65)
                             [:, tt, half * 8:half * 8 + 8, 0:64],
                             self.A(bk).rearrange("p (h c) -> p h c", c=64)),
                         waits=[("T", tv)], inc=incs)
                    self.rV[half].release()
                    if it == 3:
                        self.rW[half].release()
            P.op("tensor", lambda e: e.drain(), inc=self.rYt.s)
            self.rYt.release()
            P.wait("vector", [self.rYt.wfree()])
        P.barrier()

        if DBG in ("a_prep", "noln"):
            self.dump_dbg(bi, DBG_BUF)
            return
        self.dma_g(lambda: self.wsl(0, 2048).bitcast(F32),
                   lambda: self.dram["cos128"])
        self.dma_g(lambda: self.wsl(2048, 2048).bitcast(F32),
                   lambda: self.dram["sin128"])
        self.dma_g(lambda: self.wsl(4096, 128), lambda: self.dram["rotP"])
        self.dma_g(lambda: T["eb1b"].ap()[0:16, :]
                   .rearrange("h (c n) -> h c n", n=128),
                   lambda: self.dram["eb1"].transpose([1, 0, 2]))
        self.dma_g(lambda: T["eb1b"].ap()[32:48, :]
                   .rearrange("h (c n) -> h c n", n=128),
                   lambda: self.dram["eb1"].transpose([1, 0, 2]))
        ldcs = P.tot["LD"]
        P.wait("vector", [("LD", ldcs)])
        for qk, bufn in ((0, "abuf"), (1, "bbuf")):
            buf = T[bufn]
            for j in range(NT5):
                P.wait("tensor", [self.rQS[0].wfree(), self.rQS[1].wfree()])
                for c in range(NCH):
                    P.op("tensor",
                         lambda e, c=c, buf=buf, j=j: e.matmul(
                             self.psB.ap()[0:16, 1024:1536],
                             T["e16b"].ap()[:, c * 16:(c + 1) * 16],
                             buf.ap()[:, c * S + 512 * j: c * S + 512 * j + 512],
                             start=(c == 0), stop=(c == NCH - 1)))
                for c in range(NCH):
                    vsq = P.op("vector",
                               lambda e, c=c, buf=buf, j=j: e.tensor_tensor(
                                   T["xsq"].ap(),
                                   buf.ap()[:, c * S + 512 * j:
                                            c * S + 512 * j + 512],
                                   buf.ap()[:, c * S + 512 * j:
                                            c * S + 512 * j + 512], ALU.mult),
                               waits=[self.rXq.wfree()], inc="V")
                    P.op("tensor",
                         lambda e, c=c: e.matmul(
                             self.psB.ap()[0:16, 1536:2048],
                             T["e16b"].ap()[:, c * 16:(c + 1) * 16],
                             T["xsq"].ap(),
                             start=(c == 0), stop=(c == NCH - 1)),
                         waits=[("V", vsq)], inc=self.rXq.s)
                    self.rXq.release()
                tst = P.op("tensor", lambda e: e.drain(), inc="T")
                P.op("vector", lambda e: e.tensor_copy(
                    T["hSm"].ap(), self.psB.ap()[0:16, 1024:1536]),
                    waits=[("T", tst)], inc=self.rQS[0].s)
                self.rQS[0].release()
                P.op("vector", lambda e: e.tensor_tensor(
                    T["hR"].ap(), T["hSm"].ap(), T["hSm"].ap(), ALU.mult))
                P.op("vector", lambda e: e.scalar_tensor_tensor(
                    T["hR"].ap(), self.psB.ap()[0:16, 1536:2048], 64.0,
                    T["hR"].ap(), ALU.mult, ALU.subtract), inc=self.rQS[1].s)
                self.rQS[1].release()
                vr = P.op("vector", lambda e: e.drain(), inc="V")
                P.op("scalar", lambda e: e.activation(
                    T["hR"].ap(), T["hR"].ap(), AF.Ln,
                    bias=self.T["cbias"].ap()[0:16, 5:6],
                    scale=1.0), waits=[("V", vr)])
                cr = P.op("scalar", lambda e: e.activation(
                    T["hR"].ap(), T["hR"].ap(), AF.Exp,
                    bias=self.T["cbias"].ap()[0:16, 6:7], scale=-0.5), inc="C")
                P.op("vector", lambda e: e.tensor_copy(
                    T["hRb"].ap()[0:16, :], T["hR"].ap()), waits=[("C", cr)])
                vmr = P.op("vector", lambda e: e.scalar_tensor_tensor(
                    T["hRb"].ap()[32:48, :], T["hSm"].ap(), 1.0 / 64.0,
                    T["hR"].ap(), ALU.mult, ALU.mult), inc="V")
                wa = T["pcol"].ap()[:, 8 + 2 * qk: 9 + 2 * qk]
                ba = T["pcol"].ap()[:, 9 + 2 * qk: 10 + 2 * qk]
                for c in range(NCH):
                    self.bcast_h(self.A(2), c, 0,
                                 waits=[("V", vmr), self.rQB[0].wfree()])
                    tb2 = self.bcast_h(self.A(3), c, 32,
                                       waits=[self.rQB[1].wfree()])
                    cols = slice(c * S + 512 * j, c * S + 512 * j + 512)
                    P.op("vector",
                         lambda e, buf=buf, cols=cols: e.tensor_tensor(
                             T["t1b"].ap(), buf.ap()[:, cols], self.A(2),
                             ALU.mult),
                         waits=[("T", tb2), self.rT12[0].wfree()],
                         inc=self.rQB[0].s)
                    self.rQB[0].release()
                    vz = P.op("vector",
                              lambda e: e.tensor_tensor(
                                  T["t2b"].ap(), T["t1b"].ap(), self.A(3),
                                  ALU.subtract),
                              inc=[("V", 1), (self.rQB[1].s, 1)])
                    self.rQB[1].release()
                    cz = P.op("scalar",
                              lambda e, wa=wa, ba=ba: e.activation(
                                  T["zt"].ap(), T["t2b"].ap(), AF.Identity,
                                  bias=ba, scale=wa),
                              waits=[("V", vz), self.rT12[1].wfree()],
                              inc=[("C", 1), (self.rT12[0].s, 1)])
                    self.rT12[0].release()
                    bkr = c % 2
                    trm = P.op("tensor",
                               lambda e, bkr=bkr: e.matmul(
                                   self.A(bkr), self.wsl(4096, 128),
                                   T["zt"].ap(), start=True, stop=True),
                               waits=[("C", cz), self.rGem[bkr].wfree()],
                               inc="T")
                    P.op("vector",
                         lambda e, j=j: e.tensor_tensor(
                             T["ropA"].ap(), T["zt"].ap(),
                             self.wsl(512 * j, 512), ALU.mult),
                         waits=[("C", cz)])
                    P.op("vector",
                         lambda e, j=j, bkr=bkr: e.tensor_tensor(
                             T["zt"].ap(), self.A(bkr),
                             self.wsl(2048 + 512 * j, 512), ALU.mult),
                         waits=[("T", trm)], inc=self.rGem[bkr].s)
                    self.rGem[bkr].release()
                    P.op("vector",
                         lambda e, buf=buf, cols=cols: e.tensor_tensor(
                             buf.ap()[:, cols], T["ropA"].ap(),
                             T["zt"].ap(), ALU.add),
                         inc=self.rT12[1].s)
                    self.rT12[1].release()
        P.barrier()

        if DBG == "a_ln":
            self.dump_dbg(bi, DBG_BUF)
            return
        ldp = self.load_w8("wAproj", 0, 1024)
        P.wait("tensor", [("LD", ldp)])
        for jt in range(DBG_NJT if DBG_NJT else NT1):
            q0 = 128 * jt
            prev = None
            pend = None
            for hp in range(10):
                if hp < 8:
                    kts = [k for k in (jt - 2, jt - 1, jt) if k >= 0]
                    nk = len(kts)
                    ssl = hp % 2
                    sb0 = 2 * ssl
                    scw = [self.rSc[ssl].wfree()]
                    if hp < 2:
                        scw += [self.rGem[0].wfree(), self.rGem[1].wfree()]
                    P.wait("tensor", scw)
                    for hh in range(2):
                        for i, kt in enumerate(kts):
                            col = (sb0 + hh) * 512 + i * 128
                            P.op("tensor",
                                 lambda e, hh=hh, hp=hp, kt=kt, q0=q0, col=col:
                                 e.matmul(
                                     self.psA.ap()[:, col:col + 128],
                                     T["bbuf"].ap()[64 * hh:64 * hh + 64,
                                                    hp * S + kt * 128:
                                                    hp * S + kt * 128 + 128],
                                     T["abuf"].ap()[64 * hh:64 * hh + 64,
                                                    hp * S + q0:
                                                    hp * S + q0 + 128],
                                     start=True, stop=True,
                                     tile_position=(64 * hh, 0)),
                                 inc="T" if (hh == 1 and i == nk - 1) else None)
                    tsc = P.tot["T"]
                    psl = hp % 2
                    pof = psl * 1024
                    P.op("scalar",
                         lambda e, nk=nk, pof=pof, sb0=sb0: e.activation(
                             T["probs"].ap()[:, pof: pof + nk * 128],
                             self.psA.ap()[:, sb0 * 512: sb0 * 512 + nk * 128],
                             AF.Exp, bias=self.T["cbias"].ap()[:, 0:1],
                             scale=0.125),
                         waits=[("T", tsc), self.rPr[psl].wfree()])
                    ce = P.op("scalar",
                              lambda e, nk=nk, pof=pof, sb0=sb0: e.activation(
                                  T["probs"].ap()[:, pof + 512:
                                                  pof + 512 + nk * 128],
                                  self.psA.ap()[:, (sb0 + 1) * 512:
                                                (sb0 + 1) * 512 + nk * 128],
                                  AF.Exp, bias=self.T["cbias"].ap()[:, 0:1],
                                  scale=0.125),
                              inc=[("C", 1), (self.rSc[ssl].s, 1)])
                    self.rSc[ssl].release()
                    for hh in range(2):
                        base = pof + hh * 512
                        if nk == 3:
                            P.op("vector",
                                 lambda e, base=base: e.tensor_tensor(
                                     T["probs"].ap()[:, base:base + 128],
                                     T["probs"].ap()[:, base:base + 128],
                                     T["maskU"].ap(), ALU.mult),
                                 waits=[("C", ce)] if hh == 0 else [])
                        P.op("vector",
                             lambda e, base=base, nk=nk: e.tensor_tensor(
                                 T["probs"].ap()[:, base + (nk - 1) * 128:
                                                 base + nk * 128],
                                 T["probs"].ap()[:, base + (nk - 1) * 128:
                                                 base + nk * 128],
                                 T["maskL"].ap(), ALU.mult),
                             waits=[("C", ce)] if (hh == 0 and nk != 3) else [],
                             inc="V" if hh == 1 else None)
                    cur = (hp, kts, nk, psl, pof, P.tot["V"])
                else:
                    cur = None
                if prev is not None:
                    php, pkts, pnk, ppsl, ppof, pvpm = prev
                    ob = 512 * (php % 2)
                    P.wait("tensor", [("V", pvpm), self.rOT.wfree()])
                    for hh in range(2):
                        h = 2 * php + hh
                        for i, kt in enumerate(pkts):
                            P.op("tensor",
                                 lambda e, h=h, kt=kt, i=i, pnk=pnk, ppof=ppof,
                                 ob=ob, hh=hh: e.matmul(
                                     self.psB.ap()[0:65, ob + hh * 128:
                                                   ob + hh * 128 + 128],
                                     T["vbuf"].ap()[:, (kt * 16 + h) * 65:
                                                    (kt * 16 + h) * 65 + 65],
                                     T["probs"].ap()[:, ppof + hh * 512 +
                                                     i * 128:
                                                     ppof + hh * 512 +
                                                     i * 128 + 128],
                                     start=(i == 0), stop=(i == pnk - 1)),
                                 inc=([("T", 1), (self.rPr[ppsl].s, 1)]
                                      if (hh == 1 and i == pnk - 1) else None))
                    self.rPr[ppsl].release()
                    npend = (php, ob, P.tot["T"])
                else:
                    npend = None
                if pend is not None:
                    php, ob, tpv = pend
                    rr = T["rowsC"].ap()[0:1, 0:256]
                    P.op("vector",
                         lambda e, php=php, ob=ob, rr=rr: e.tensor_tensor(
                             rr, self.psB.ap()[64:65, ob:ob + 256],
                             T["sinkrow"].ap().rearrange("o (h r) -> o h r",
                                                         r=1)
                             [:, 2 * php:2 * php + 2, :]
                             .broadcast_to([1, 2, 128]),
                             ALU.add),
                         waits=[("T", tpv), self.rRowA.wfree()])
                    vr = P.op("vector", lambda e, rr=rr: e.reciprocal(rr, rr),
                              inc="V")
                    P.op("tensor",
                         lambda e: e.matmul(
                             self.psB.ap()[0:64, 1024:1152],
                             T["onesrf"].ap()[0:1, :],
                             T["rowsC"].ap()[0:1, 0:128],
                             start=True, stop=True),
                         waits=[("V", vr), self.rBc.wfree()])
                    tb2 = P.op("tensor",
                               lambda e: e.matmul(
                                   self.psB.ap()[64:128, 1024:1152],
                                   T["onesrf"].ap()[0:1, :],
                                   T["rowsC"].ap()[0:1, 128:256],
                                   start=True, stop=True,
                                   tile_position=(0, 64)),
                               inc=[("T", 1), (self.rRowA.s, 1)])
                    self.rRowA.release()
                    P.op("scalar",
                         lambda e, ob=ob, php=php: e.activation(
                             T["outp"].ap()[0:64, php * 128:(php + 1) * 128],
                             self.psB.ap()[0:64, ob:ob + 128],
                             AF.Copy, bias=0.0, scale=1.0),
                         waits=[("T", tb2)])
                    c2 = P.op("scalar",
                              lambda e, ob=ob, php=php: e.activation(
                                  T["outp"].ap()[64:128,
                                                 php * 128:(php + 1) * 128],
                                  self.psB.ap()[0:64, ob + 128:ob + 256],
                                  AF.Copy, bias=0.0, scale=1.0),
                              inc=[("C", 1), (self.rOT.s, 1)])
                    self.rOT.release()
                    P.op("vector",
                         lambda e, php=php: e.tensor_tensor(
                             T["outp"].ap()[:, php * 128:(php + 1) * 128],
                             T["outp"].ap()[:, php * 128:(php + 1) * 128],
                             self.psB.ap()[:, 1024:1152], ALU.mult),
                         waits=[("C", c2)],
                         inc=[(self.rBc.s, 1), (self.rOp.s, 1)])
                    self.rBc.release()
                prev = cur
                pend = npend
            vop = P.tot[self.rOp.s]
            P.wait("tensor", [self.rSc[0].wfree(), self.rSc[1].wfree(),
                              (self.rOp.s, vop)])
            for dc in range(NCH):
                bk = dc % 2
                for c in range(NCH):
                    P.op("tensor",
                         lambda e, c=c, dc=dc, bk=bk: e.matmul(
                             self.psA.ap()[:, bk * 512: bk * 512 + 128],
                             self.wsl(c * 1024 + dc * 128, 128),
                             T["outp"].ap()[:, c * 128:(c + 1) * 128],
                             start=(c == 0), stop=(c == NCH - 1)),
                         waits=[self.rGem[bk].wfree()] if c == 0 else [],
                         inc="T" if c == NCH - 1 else None)
                self._res_evac(dc, jt, bi, final, P.tot["T"],
                               self.psA.ap()[:, bk * 512: bk * 512 + 128],
                               self.rGem[bk], n=128,
                               bias_col=T["pcol"].ap()[:, dc:dc + 1])
            P.op("tensor", lambda e: e.drain(), inc=self.rOp.s)
            self.rOp.release()
        P.barrier()

    def dump_dbg(self, bi, bufn="abuf"):
        P = self.P
        P.barrier()
        if bi == 0:
            src_t = self.T[bufn]
            for c in range(NCH):
                self.dma_g(lambda c=c: self.yout[0, c],
                           lambda c=c, src_t=src_t: src_t.ap()[:, c * S:(c + 1) * S],
                           inc="YD")
        P.wait("gpsimd", [("YD", P.tot.get("YD", 0))])
        P.barrier()

    def bcast_h(self, dst, c, r0, waits=()):
        return self.P.op("tensor",
                         lambda e: e.matmul(
                             dst,
                             self.T["eb1b"].ap()[r0:r0 + 16,
                                                 c * 128:(c + 1) * 128],
                             self.T["hRb"].ap()[r0:r0 + 16, :],
                             start=True, stop=True),
                         waits=waits, inc="T")

    def _res_evac(self, dc, jt, bi, final, tmm, ps, gres, n=128, bias_col=None):
        P, T = self.P, self.T
        t0 = jt * n

        def mk(dst):
            if bias_col is not None:
                return lambda e: e.scalar_tensor_tensor(
                    dst, ps, bias_col, self.sch(dc, t0, n), ALU.add, ALU.add)
            return lambda e: e.scalar_tensor_tensor(
                dst, ps, 1.0, self.sch(dc, t0, n), ALU.mult, ALU.add)
        if final:
            osl = 0
            P.op("vector", mk(T["outsb"].ap()[:, osl * 512: osl * 512 + n]),
                 waits=[("T", tmm), self.rOut[osl].wfree()],
                 inc=[(gres.s, 1), ("V", 1)])
            gres.release()
            vv = P.tot["V"]
            P.wait("sync", [("V", vv)])
            self.dma_s(lambda dc=dc, t0=t0, n=n, bi=bi: self.yout[bi, dc][:, t0:t0 + n],
                lambda osl=osl, n=n: T["outsb"].ap()[:, osl * 512: osl * 512 + n],
                inc="YD")
            P.op("sync", lambda e: e.drain(), waits=[("YD", P.tot["YD"])],
                 inc=self.rOut[osl].s)
            self.rOut[osl].release()
        else:
            P.op("vector", mk(self.sch(dc, t0, n)),
                 waits=[("T", tmm)], inc=gres.s)
            gres.release()

    def ffn(self, bi, final):
        P, T = self.P, self.T
        self.stats_plain("Rrow", "MRrow", EPS_P)
        P.barrier()

        P.op("vector", lambda e: e.memset(
            T["vbuf"].ap().rearrange("p (a c) -> p a c", c=65)[:, :, 64:65], 1.0),
            inc="V")

        gateP = T["bbuf"].ap()
        gfeat = T["qbbuf"].ap()[0:8, 0:2048]

        ldkv = self.load_w8("wFkv", 0, 1024)
        for c in range(NCH):
            self.dma_g(lambda c=c: T["expkb"].ap()[:, c * 8:(c + 1) * 8],
                       lambda c=c: self.wdram("wFs")[c])
        for c in range(NCH):
            self.dma_s(lambda c=c: T["rowsC"].ap()[:, c * 8:(c + 1) * 8],
                       lambda c=c: self.wdram("wFs")[c], inc="WL")
        lds = P.tot["LD"]
        wls = P.tot["WL"]
        P.wait("tensor", [("LD", lds)])
        P.wait("vector", [("LD", lds), ("WL", wls)])
        vwl = P.op("vector", lambda e: e.tensor_tensor(
            T["expkb"].ap()[:, 64:128], T["rowsC"].ap()[:, 0:64],
            T["expkb"].ap()[:, 0:64], ALU.subtract), inc="V")
        for g in range(2):
            for c in range(NCH):
                P.op("tensor", lambda e, g=g, c=c: e.matmul(
                    self.psB.ap()[0:1, 1024:1032], T["ones1b"].ap(),
                    T["expkb"].ap()[:, 64 * g + c * 8:64 * g + (c + 1) * 8],
                    start=(g == 0 and c == 0), stop=(g == 1 and c == NCH - 1)),
                    waits=[("V", vwl)] if (g == 0 and c == 0) else [],
                    inc="T" if (g == 1 and c == NCH - 1) else None)
        twb = P.tot["T"]
        cwb = P.op("scalar", lambda e: e.activation(
            T["expkb"].ap()[0:1, 128:136], self.psB.ap()[0:1, 1024:1032],
            AF.Copy, bias=0.0, scale=1.0 / 1024.0),
            waits=[("T", twb)], inc=[("C", 1), (self.rGt.s, 1)])
        self.rGt.release()
        PBT = [(self.psB, 256), (self.psB, 768), (self.psA, 1280),
               (self.psA, 1792)]
        for jj in range(NT5):
            ps, base = PBT[jj]
            for tl in range(4):
                for ri, rname in ((0, "Rrow"), (4, "rowsB")):
                    P.op("tensor",
                         lambda e, jj=jj, tl=tl, ri=ri, rname=rname,
                         ps=ps, base=base: e.matmul(
                             ps.ap()[:, base + ri + tl:base + ri + tl + 1],
                             T[rname].ap()[32 * jj:32 * jj + 1,
                                           tl * 128:tl * 128 + 128],
                             T["ones1b"].ap()[32 * jj:32 * jj + 1, 0:1],
                             start=True, stop=True,
                             tile_position=(32 * jj, 0)))
        P.op("tensor", lambda e: e.matmul(
            self.psB.ap()[:, 1600:1608], T["onesrb"].ap()[0:1, :],
            T["expkb"].ap()[0:1, 128:136], start=True, stop=True),
            waits=[("C", cwb)], inc="T")
        ttr = P.tot["T"]
        for jj in range(NT5):
            ps, base = PBT[jj]
            P.op("vector", lambda e, ps=ps, base=base, jj=jj: e.tensor_copy(
                T["rowsC"].ap()[:, 64 + 8 * jj:72 + 8 * jj],
                ps.ap()[:, base:base + 8]),
                waits=[("T", ttr)] if jj == 0 else [])
        P.op("vector", lambda e: e.tensor_copy(
            T["rowsC"].ap()[:, 96:104], self.psB.ap()[:, 1600:1608]))
        P.op("vector", lambda e: e.drain(), inc="V")
        vcp = P.tot["V"]
        P.wait("tensor", [("V", vcp)])
        for j in range(NT5):
            yv = self.ytile(j, "Rrow", "MRrow",
                            lambda c, j=j: self.ach(c, 512 * j, 512))
            for oc in range(4):
                bk = oc % 2
                for c in range(NCH):
                    P.op("tensor",
                         lambda e, c=c, oc=oc, bk=bk, j=j: e.matmul(
                             self.A(bk), self.wsl(c * 1024 + oc * 128, 128),
                             self.ach(c, 512 * j, 512),
                             start=(c == 0), stop=(c == NCH - 1)),
                         waits=([self.rGem[bk].wfree()] +
                                ([("V", yv)] if oc == 0 else []))
                         if c == 0 else [],
                         inc="T" if c == NCH - 1 else None)
                tk = P.tot["T"]
                P.op("scalar",
                     lambda e, oc=oc, bk=bk, j=j: e.activation(
                         self.kch(oc, 512 * j, 512), self.A(bk), AF.Copy,
                         bias=0.0, scale=1.0),
                     waits=[("T", tk)], inc=self.rGem[bk].s)
                self.rGem[bk].release()
            for it in range(4):
                tt = 4 * j + it
                bk = 2 + it % 2
                for c in range(NCH):
                    P.op("tensor",
                         lambda e, c=c, it=it, j=j, bk=bk: e.matmul(
                             self.A(bk),
                             self.ach(c, 512 * j + it * 128, 128),
                             self.wsl(c * 1024 + 512, 512),
                             start=(c == 0), stop=(c == NCH - 1)),
                         waits=[self.rV[it % 2].wfree(), self.rGt.wfree()]
                         if c == 0 else [])
                    P.op("tensor",
                         lambda e, c=c, it=it, j=j: e.matmul(
                             self.psB.ap()[0:128, 1024 + 8 * it:
                                           1032 + 8 * it],
                             self.sch(c, 512 * j + it * 128, 128),
                             T["expkb"].ap()[:, c * 8:(c + 1) * 8],
                             start=(c == 0), stop=False))
                for c in range(NCH):
                    P.op("tensor",
                         lambda e, c=c, it=it, j=j: e.matmul(
                             self.psB.ap()[0:128, 1024 + 8 * it:
                                           1032 + 8 * it],
                             self.sch(c, 512 * j + it * 128, 128),
                             T["expkb"].ap()[:, 64 + c * 8:64 + (c + 1) * 8],
                             start=False, stop=(c == NCH - 1)),
                         inc="T" if c == NCH - 1 else None)
                tv = P.tot["T"]
                P.op("vector",
                     lambda e, tt=tt, bk=bk: e.tensor_copy(
                         T["vbuf"].ap().rearrange("p (t h c) -> p t h c",
                                                  t=NT1, h=16, c=65)
                         [:, tt, 0:8, 0:64],
                         self.A(bk).rearrange("p (h c) -> p h c", c=64)),
                     waits=[("T", tv)], inc=self.rV[it % 2].s)
                self.rV[it % 2].release()
                rcS = T["rowsC"].ap()[:, 68 + 8 * j + it:69 + 8 * j + it]
                P.op("vector",
                     lambda e, rcS=rcS, it=it: e.scalar_tensor_tensor(
                         T["rowsC"].ap()[:, 136 + 8 * it:144 + 8 * it],
                         T["rowsC"].ap()[:, 96:104], rcS,
                         self.psB.ap()[0:128, 1024 + 8 * it:1032 + 8 * it],
                         ALU.mult, ALU.subtract),
                     waits=[("T", tv)], inc=self.rGt.s)
                self.rGt.release()
            gs32 = T["rowsC"].ap()[:, 136:168]
            P.op("vector", lambda e: e.drain())
            for it2 in range(4):
                rcR = T["rowsC"].ap()[:, 64 + 8 * j + it2:65 + 8 * j + it2]
                P.op("vector",
                     lambda e, rcR=rcR, it2=it2: e.tensor_scalar(
                         T["rowsC"].ap()[:, 136 + 8 * it2:144 + 8 * it2],
                         T["rowsC"].ap()[:, 136 + 8 * it2:144 + 8 * it2],
                         rcR, None, ALU.mult),
                     inc="V" if it2 == 3 else None)
            vsc = P.tot["V"]
            cg = P.op("scalar",
                      lambda e, gs32=gs32: e.activation(
                          gs32, gs32, AF.Exp, bias=0.0, scale=1.0),
                      waits=[("V", vsc)], inc="C")
            P.op("vector", lambda e, gs32=gs32: e.tensor_scalar(
                gs32, gs32, 1.0, None, ALU.add), waits=[("C", cg)])
            P.op("vector", lambda e: e.drain())
            P.op("vector", lambda e, gs32=gs32: e.reciprocal(gs32, gs32))
            P.op("vector", lambda e: e.drain())
            for it2 in range(4):
                P.op("vector", lambda e, it2=it2: e.max(
                    T["rowsC"].ap()[:, 104 + 8 * it2:112 + 8 * it2],
                    T["rowsC"].ap()[:, 136 + 8 * it2:144 + 8 * it2]))
            P.op("vector", lambda e: e.drain())
            for it2 in range(4):
                P.op("vector", lambda e, it2=it2: e.tensor_scalar(
                    T["rowsC"].ap()[:, 104 + 8 * it2:112 + 8 * it2],
                    T["rowsC"].ap()[:, 136 + 8 * it2:144 + 8 * it2],
                    T["rowsC"].ap()[:, 106 + 8 * it2:107 + 8 * it2], None,
                    ALU.is_ge))
            P.op("vector", lambda e: e.drain())
            for it2 in range(4):
                P.op("vector", lambda e, it2=it2: e.tensor_tensor(
                    T["gtmp"].ap()[:, it2 * 8:(it2 + 1) * 8],
                    T["rowsC"].ap()[:, 136 + 8 * it2:144 + 8 * it2],
                    T["rowsC"].ap()[:, 104 + 8 * it2:112 + 8 * it2],
                    ALU.mult), inc="V" if it2 == 3 else None)
            vg = P.tot["V"]
            for h in range(8):
                tg = P.op("tensor",
                          lambda e, h=h: e.transpose(
                              self.psB.ap()[0:4, 1536:1600].bitcast(BF16),
                              T["gtmp"].ap()
                              .rearrange("p (a b) -> p a b", b=8)[:, :, h],
                              T["identb"].ap()),
                          waits=[("V", vg), self.rBc.wfree()], inc="T")
                vv = P.op("vector",
                          lambda e: e.tensor_copy(
                              T["gtr"].ap()[:, 0:128],
                              self.psB.ap()[0:4, 1536:1600].bitcast(BF16)),
                          waits=[("T", tg), ("WL", P.tot.get("WL", 0))],
                          inc=[("V", 1), (self.rBc.s, 1)])
                self.rBc.release()
                P.wait("sync", [("V", vv)])
                self.dma_s(lambda h=h, j=j: gateP[32 * (h % 2):32 * (h % 2) + 1,
                                                  (h // 2) * 2048 + 512 * j:
                                                  (h // 2) * 2048 + 512 * j + 512]
                           .rearrange("o (a b) -> o a b", a=4),
                           lambda: T["gtr"].ap()[:, 0:128], inc="WL")
                self.dma_s(lambda h=h, j=j: gfeat[h:h + 1,
                                                  512 * j:512 * j + 512]
                           .rearrange("o (a b) -> o a b", a=4),
                           lambda: T["gtr"].ap()[:, 0:128], inc="WL")
                P.wait("vector", [("WL", P.tot["WL"])])
        P.barrier()

        if DBG == "f_prep":
            P.barrier()
            if bi == 0:
                for c in range(4):
                    self.dma_g(lambda c=c: self.yout[0, c],
                               lambda c=c: self.kch(c, 0, S), inc="YD")
                self.dma_g(lambda: self.yout[0, 4],
                           lambda: T["qbbuf"].ap()[:, 0:2048], inc="YD")
                self.dma_g(lambda: self.yout[0, 5],
                           lambda: T["bbuf"].ap()[:, 0:2048], inc="YD")
                self.dma_g(lambda: self.yout[0, 6],
                           lambda: T["bbuf"].ap()[:, 2048:4096], inc="YD")
            P.wait("gpsimd", [("YD", P.tot.get("YD", 0))])
            P.barrier()
            return
        ldq = self.load_w8("wFq", 0, 512)
        ldo = self.load_w8("wFo", 4096, 1024, nch=4)
        self.dma_g(lambda: T["boFb"].ap(), lambda: self.dram["boF"])
        P.wait("tensor", [("LD", P.tot["LD"]), ("WL", P.tot["WL"])])
        for jq in range(NT5):
            q0 = 512 * jq
            for oc in range(4):
                bk = oc % 2
                for c in range(NCH):
                    P.op("tensor",
                         lambda e, c=c, oc=oc, bk=bk, q0=q0: e.matmul(
                             self.A(bk), self.wsl(c * 512 + oc * 128, 128),
                             self.ach(c, q0, 512),
                             start=(c == 0), stop=(c == NCH - 1)),
                         waits=[self.rGem[bk].wfree(), self.rQt.wfree(),
                                self.rSc[bk].wfree()]
                         if c == 0 else [],
                         inc="T" if c == NCH - 1 else None)
                tq = P.tot["T"]
                P.op("scalar",
                     lambda e, oc=oc, bk=bk: e.activation(
                         T["qbuf"].ap()[:, oc * 512:(oc + 1) * 512],
                         self.A(bk), AF.Copy, bias=0.0, scale=1.0),
                     waits=[("T", tq)], inc=self.rGem[bk].s)
                self.rGem[bk].release()
            cq = P.op("scalar", lambda e: e.drain(), inc="C")
            for hp in range(4):
                self._attn_core(hp=hp, jq=jq, kbuf=self.kch, qwait=cq,
                                scale=0.125, shiftcol=1, aff=False,
                                gate=(gateP, 2048), nkt=NT1)
            self._flush_epi()
            vop = P.tot[self.rOp.s]
            P.wait("tensor", [self.rSc[0].wfree(), self.rSc[1].wfree(),
                              (self.rOp.s, vop)])
            for dc in range(NCH):
                bk = dc % 2
                for hc in range(4):
                    P.op("tensor",
                         lambda e, hc=hc, dc=dc, bk=bk: e.matmul(
                             self.A(bk),
                             self.wsl(4096 + hc * 1024 + dc * 128, 128),
                             T["outp"].ap()[:, hc * 512:(hc + 1) * 512],
                             start=(hc == 0), stop=False),
                         waits=[self.rGem[bk].wfree()] if hc == 0 else [])
                P.op("tensor",
                     lambda e, dc=dc, bk=bk, q0=q0: e.matmul(
                         self.A(bk), T["boFb"].ap()[:, dc * 128:(dc + 1) * 128],
                         gfeat[:, q0:q0 + 512],
                         start=False, stop=True),
                     inc="T")
                self._res_evac(dc, jq, bi, final, P.tot["T"],
                               self.A(bk), self.rGem[bk], n=512)
            P.op("tensor", lambda e: e.drain(), inc=self.rOp.s)
            self.rOp.release()
        P.barrier()

    def _pv(self, hp, kt, ce, nkt):
        P, T = self.P, self.T
        P.wait("tensor", [("C", ce)])
        psl = kt % 2
        pof = psl * 1024
        for hh in range(2):
            h = 2 * hp + hh
            P.op("tensor",
                 lambda e, h=h, kt=kt, hh=hh, pof=pof: e.matmul(
                     self.psB.ap()[0:65, hh * 512:hh * 512 + 512],
                     T["vbuf"].ap()[:, (kt * 16 + h) * 65:
                                    (kt * 16 + h) * 65 + 65],
                     T["probs"].ap()[:, pof + hh * 512:pof + hh * 512 + 512],
                     start=(kt == 0), stop=(kt == nkt - 1)),
                 inc=([("T", 1), (self.rPr[psl].s, 1)] if hh == 1 else None))
        self.rPr[psl].release()

    def _attn_core(self, hp, jq, kbuf, qwait, scale, shiftcol, aff, gate, nkt):
        P, T = self.P, self.T
        q0 = 512 * jq
        ce_q = []
        for kt in range(nkt):
            sb0 = 2 * (kt % 2)
            if kt == 0:
                scw = [self.rSc[0].wfree(), self.rSc[1].wfree()]
                if hp == 0:
                    scw += [self.rGem[0].wfree(), self.rGem[1].wfree()]
                P.wait("tensor", scw)
            for hh in range(2):
                P.op("tensor",
                     lambda e, hh=hh, hp=hp, kt=kt, sb0=sb0: e.matmul(
                         self.A(sb0 + hh),
                         kbuf(hp, kt * 128, 128)[64 * hh:64 * hh + 64, :],
                         T["qbuf"].ap()[64 * hh:64 * hh + 64,
                                        hp * 512:(hp + 1) * 512],
                         start=True, stop=(not aff),
                         tile_position=(64 * hh, 0)),
                     waits=[("C", qwait)] if (kt == 0 and hh == 0) else [])
            if aff:
                for hh in range(2):
                    h = 2 * hp + hh
                    rb = 64 * (h % 2) + 32 * ((h % 4) // 2)
                    P.op("tensor",
                         lambda e, h=h, rb=rb, kt=kt, hh=hh, sb0=sb0: e.matmul(
                             self.A(sb0 + hh),
                             T["kbbuf"].ap()[rb:rb + 32,
                                             (h // 4) * S + kt * 128:
                                             (h // 4) * S + kt * 128 + 128],
                             T["qbbuf"].ap()[rb:rb + 32,
                                             (h // 4) * 512:
                                             (h // 4) * 512 + 512],
                             start=False, stop=True,
                             tile_position=(rb, 0)),
                         inc="T" if hh == 1 else None)
            else:
                P.op("tensor", lambda e: e.drain(), inc="T")
            tsc = P.tot["T"]
            psl = kt % 2
            pof = psl * 1024
            ce = P.op("scalar",
                      lambda e, pof=pof, sb0=sb0, scale=scale,
                      shiftcol=shiftcol: e.activation(
                          T["probs"].ap()[:, pof:pof + 1024],
                          self.psA.ap()[:, sb0 * 512: sb0 * 512 + 1024],
                          AF.Exp, bias=T["cbias"].ap()[:, shiftcol:shiftcol + 1],
                          scale=scale),
                      waits=[("T", tsc), self.rPr[psl].wfree()],
                      inc=[("C", 1), (self.rSc[kt % 2].s, 1)])
            self.rSc[kt % 2].release()
            ce_q.append(ce)
            if kt == 1:
                self._flush_epi()
                P.wait("tensor", [self.rOT.wfree()])
            if kt > 0:
                self._pv(hp, kt - 1, ce_q[kt - 1], nkt)
        self._pv(hp, nkt - 1, ce_q[nkt - 1], nkt)
        self._epi_pend = (hp, jq, gate, P.tot["T"])

    def _flush_epi(self):
        if self._epi_pend is None:
            return
        hp, jq, gate, tpv = self._epi_pend
        self._epi_pend = None
        self._attn_epilogue(hp, jq, gate, tpv)

    def _attn_epilogue(self, hp, jq, gate, tpv):
        P, T = self.P, self.T
        q0 = 512 * jq
        s1 = T["rowsC"].ap()[0:1, :]
        s2 = T["rowsC"].ap()[32:33, :]
        P.wait("vector", [("T", tpv), self.rRowA.wfree()])
        P.op("vector", lambda e, s1=s1: e.reciprocal(
            s1, self.psB.ap()[64:65, 0:512]))
        vr = P.op("vector", lambda e, s2=s2: e.reciprocal(
            s2, self.psB.ap()[64:65, 512:1024]), inc="V")
        if gate is not None:
            gp, gstride = gate
            for hh, sr in ((0, s1), (1, s2)):
                h = 2 * hp + hh
                P.op("vector",
                     lambda e, h=h, sr=sr, gp=gp, gstride=gstride, q0=q0:
                     e.tensor_tensor(
                         sr, sr, gp[32 * (h % 2):32 * (h % 2) + 1,
                                    (h // 2) * gstride + q0:
                                    (h // 2) * gstride + q0 + 512], ALU.mult),
                     inc="V" if hh == 1 else None)
            vr = P.tot["V"]
        P.op("tensor",
             lambda e, s1=s1: e.matmul(
                 self.psB.ap()[0:64, 1024:1536],
                 T["onesrf"].ap()[0:1, :],
                 s1, start=True, stop=True),
             waits=[("V", vr), self.rBc.wfree()])
        tb2 = P.op("tensor",
                   lambda e, s2=s2: e.matmul(
                       self.psB.ap()[64:128, 1024:1536],
                       T["onesrf"].ap()[32:33, :],
                       s2, start=True, stop=True, tile_position=(32, 64)),
                   inc=[("T", 1), (self.rRowA.s, 1)])
        self.rRowA.release()
        hc = hp
        P.op("scalar",
             lambda e, hc=hc: e.activation(
                 T["outp"].ap()[0:64, hc * 512:(hc + 1) * 512],
                 self.psB.ap()[0:64, 0:512], AF.Copy, bias=0.0, scale=1.0),
             waits=[("T", tb2)])
        c2 = P.op("scalar",
                  lambda e, hc=hc: e.activation(
                      T["outp"].ap()[64:128, hc * 512:(hc + 1) * 512],
                      self.psB.ap()[0:64, 512:1024], AF.Copy, bias=0.0,
                      scale=1.0),
                  inc=[("C", 1), (self.rOT.s, 1)])
        self.rOT.release()
        P.op("vector",
             lambda e, hc=hc: e.tensor_tensor(
                 T["outp"].ap()[:, hc * 512:(hc + 1) * 512],
                 T["outp"].ap()[:, hc * 512:(hc + 1) * 512],
                 self.psB.ap()[:, 1024:1536], ALU.mult),
             waits=[("C", c2)],
             inc=[(self.rBc.s, 1), (self.rOp.s, 1)])
        self.rBc.release()

    def attn_b(self, bi, final):
        P, T = self.P, self.T
        vst = self.stats_plain("Rrow", "MRrow", EPS_P, hold_rows=True)
        P.op("vector", lambda e: e.tensor_tensor(
            T["rowsA"].ap(), T["rowsA"].ap(), T["rowsA"].ap(), ALU.mult),
            waits=[("V", vst)])
        P.op("vector", lambda e: e.tensor_scalar(
            T["rowsA"].ap(), T["rowsA"].ap(), -EPS_P, 1.0 + EPS_2,
            ALU.mult, ALU.add))
        P.op("vector", lambda e: e.drain())
        P.op("vector", lambda e: e.tensor_scalar(
            T["rowsA"].ap(), T["rowsA"].ap(), EPS_2, None, ALU.max), inc="V")
        vv = P.tot["V"]
        P.op("scalar", lambda e: e.activation(
            T["rowsA"].ap(), T["rowsA"].ap(), AF.Ln, bias=0.0, scale=1.0),
            waits=[("V", vv)])
        cr = P.op("scalar", lambda e: e.activation(
            T["rowsA"].ap(), T["rowsA"].ap(), AF.Exp, bias=0.0, scale=-0.5),
            inc="C")
        P.op("vector", lambda e: e.tensor_tensor(
            T["Arow"].ap(), T["Rrow"].ap(), T["rowsA"].ap(), ALU.mult),
            waits=[("C", cr)])
        P.op("vector", lambda e: e.tensor_tensor(
            T["MArow"].ap(), T["MRrow"].ap(), T["rowsA"].ap(), ALU.mult),
            inc=[("V", 1), (self.rRowA.s, 1)])
        self.rRowA.release()
        P.barrier()

        P.op("vector", lambda e: e.memset(
            T["vbuf"].ap().rearrange("p (a c) -> p a c", c=65)[:, :, 64:65], 1.0),
            inc="V")

        ldk = self.load_w8("wBk", 0, 1024)
        P.wait("tensor", [("LD", ldk)])
        for j in range(NT5):
            cv = self.y1tile(j, lambda c: T["ybuf"].ap()[:, c * 512:(c + 1) * 512])
            for oc in range(NCH):
                bk = oc % 2
                for c in range(NCH):
                    P.op("tensor",
                         lambda e, c=c, oc=oc, bk=bk: e.matmul(
                             self.A(bk), self.wsl(c * 1024 + oc * 128, 128),
                             T["ybuf"].ap()[:, c * 512:(c + 1) * 512],
                             start=(c == 0), stop=(c == NCH - 1)),
                         waits=([self.rGem[bk].wfree()] +
                                ([("C", cv)] if oc == 0 else []))
                         if c == 0 else [],
                         inc="T" if c == NCH - 1 else None)
                tk = P.tot["T"]
                incs = [(self.rGem[bk].s, 1)]
                if oc == NCH - 1:
                    incs.append((self.rYt.s, 1))
                P.op("scalar",
                     lambda e, oc=oc, bk=bk, j=j: e.activation(
                         self.ach(oc, 512 * j, 512), self.A(bk), AF.Copy,
                         bias=0.0, scale=1.0),
                     waits=[("T", tk)], inc=incs)
                self.rGem[bk].release()
            self.rYt.release()
            P.wait("vector", [self.rYt.wfree()])
            P.wait("scalar", [self.rYt.wfree()])
        P.barrier()
        ldv = self.load_w8("wBv", 0, 1024)
        P.wait("tensor", [("LD", ldv)])
        for j in range(NT5):
            cv = self.y1tile(j, lambda c: T["ybuf"].ap()[:, c * 512:(c + 1) * 512])
            for it in range(4):
                tt = 4 * j + it
                for half in range(2):
                    bk = 2 + half
                    for c in range(NCH):
                        P.op("tensor",
                             lambda e, c=c, it=it, half=half, bk=bk: e.matmul(
                                 self.A(bk),
                                 T["ybuf"].ap()[:, c * 512 + it * 128:
                                                c * 512 + it * 128 + 128],
                                 self.wsl(c * 1024 + half * 512, 512),
                                 start=(c == 0), stop=(c == NCH - 1)),
                             waits=([self.rV[half].wfree()] +
                                    ([("C", cv)] if it == 0 else []))
                             if c == 0 else [],
                             inc="T" if c == NCH - 1 else None)
                    tv = P.tot["T"]
                    incs = [(self.rV[half].s, 1)]
                    if it == 3:
                        incs.append((self.rYt.s, 1))
                    P.op("vector",
                         lambda e, tt=tt, half=half, bk=bk: e.tensor_copy(
                             T["vbuf"].ap().rearrange("p (t h c) -> p t h c",
                                                      t=NT1, h=16, c=65)
                             [:, tt, half * 8:half * 8 + 8, 0:64],
                             self.A(bk).rearrange("p (h c) -> p h c", c=64)),
                         waits=[("T", tv)], inc=incs)
                    self.rV[half].release()
            self.rYt.release()
            P.wait("vector", [self.rYt.wfree()])
            P.wait("scalar", [self.rYt.wfree()])
        P.barrier()
        self.dma_g(lambda: self.wsl(0, 256).bitcast(F32),
                   lambda: self.dram["wBhk"])
        self.dma_g(lambda: self.wsl(256, 256).bitcast(F32),
                   lambda: self.dram["wBhq"])
        ldh = P.tot["LD"]
        P.wait("tensor", [("LD", ldh)])
        for j in range(NT5):
            for g4 in range(4):
                self._cluster_softmax(
                    src=lambda hh, g4=g4, j=j: self.ach(
                        g4 * 2 + hh // 2, 512 * j, 512)
                    [64 * (hh % 2):64 * (hh % 2) + 64, :],
                    woff=0, g4=g4, j=j, is_q=False)
        P.barrier()

        self.dma_g(lambda: self.wsl(4096, 256).bitcast(F32),
                   lambda: self.dram["wBhq"])
        for jq in range(NT5):
            q0 = 512 * jq
            cv = self.y1tile(jq, lambda c: T["ybuf"].ap()[:, c * 512:(c + 1) * 512])
            for op_ in range(4):
                sl = op_ % 2
                ld = self.load_w8_sliced("wBq", sl * 2048, 256,
                                         (op_ * 256, op_ * 256 + 256),
                                         self.rW[sl])
                for oc2 in range(2):
                    oc = op_ * 2 + oc2
                    bk = oc % 2
                    for c in range(NCH):
                        P.op("tensor",
                             lambda e, c=c, oc2=oc2, sl=sl, bk=bk: e.matmul(
                                 self.A(bk),
                                 self.wsl(sl * 2048 + c * 256 + oc2 * 128, 128),
                                 T["ybuf"].ap()[:, c * 512:(c + 1) * 512],
                                 start=(c == 0), stop=(c == NCH - 1)),
                             waits=([("LD", ld), self.rGem[bk].wfree(),
                                     self.rQt.wfree(), self.rSc[bk].wfree()] +
                                    ([("C", cv)] if op_ == 0 and oc2 == 0
                                     else []))
                             if c == 0 else [],
                             inc="T" if c == NCH - 1 else None)
                    tq = P.tot["T"]
                    incs = [(self.rGem[bk].s, 1)]
                    if oc2 == 1:
                        incs.append((self.rW[sl].s, 1))
                    if oc == 7:
                        incs.append((self.rYt.s, 1))
                    P.op("scalar",
                         lambda e, oc=oc, bk=bk: e.activation(
                             T["qbuf"].ap()[:, oc * 512:(oc + 1) * 512],
                             self.A(bk), AF.Copy, bias=0.0, scale=0.125),
                         waits=[("T", tq)], inc=incs)
                    self.rGem[bk].release()
                    if oc2 == 1:
                        self.rW[sl].release()
            self.rYt.release()
            cq = P.op("scalar", lambda e: e.drain(), inc="C")
            for g4 in range(4):
                self._cluster_softmax(
                    src=lambda hh, g4=g4: T["qbuf"].ap()
                    [64 * (hh % 2):64 * (hh % 2) + 64,
                     (g4 * 2 + hh // 2) * 512:(g4 * 2 + hh // 2) * 512 + 512],
                    woff=4096, g4=g4, j=jq, is_q=True)
            for hp in range(8):
                self._attn_core(hp=hp, jq=jq, kbuf=self.ach, qwait=cq,
                                scale=1.0, shiftcol=2, aff=True, gate=None,
                                nkt=NT1)
            self._flush_epi()
            vop = P.tot[self.rOp.s]
            P.wait("tensor", [self.rSc[0].wfree(), self.rSc[1].wfree(),
                              (self.rOp.s, vop)])
            ldos = [self.load_w8_sliced(
                "wBo", 4352 + (d0 % 2) * 1024, 128,
                (d0 * 128, d0 * 128 + 128), self.rW[d0 % 2])
                for d0 in range(2)]
            for dc in range(NCH):
                sl = dc % 2
                ldo = ldos[dc]
                bk = dc % 2
                for c in range(NCH):
                    P.op("tensor",
                         lambda e, c=c, bk=bk, sl=sl: e.matmul(
                             self.A(bk), self.wsl(4352 + sl * 1024 + c * 128, 128),
                             T["outp"].ap()[:, c * 512:(c + 1) * 512],
                             start=(c == 0), stop=(c == NCH - 1)),
                         waits=[("LD", ldo), self.rGem[bk].wfree()]
                         if c == 0 else [],
                         inc="T" if c == NCH - 1 else None)
                two = P.tot["T"]
                tb = "t1b" if sl == 0 else "t2b"
                self.bcast(self.Bk(2), "Rrow", jq, 0, 512,
                           waits=[self.rYb[0].wfree()])
                tM = self.bcast(self.psB.ap()[:, 1536:2048], "MRrow", jq, 0, 512,
                                waits=[self.rYb[1].wfree()])
                P.op("vector",
                     lambda e, dc=dc, jq=jq, tb=tb: e.tensor_tensor(
                         T[tb].ap(), self.sch(dc, 512 * jq, 512),
                         self.Bk(2), ALU.mult),
                     waits=[("T", tM), self.rT12[sl].wfree()],
                     inc=self.rYb[0].s)
                self.rYb[0].release()
                P.op("vector",
                     lambda e, tb=tb: e.tensor_tensor(
                         T[tb].ap(), T[tb].ap(),
                         self.psB.ap()[:, 1536:2048], ALU.subtract),
                     inc=self.rYb[1].s)
                self.rYb[1].release()
                P.op("vector",
                     lambda e, dc=dc, jq=jq, bk=bk, tb=tb:
                     e.scalar_tensor_tensor(
                         self.bch(dc, 512 * jq, 512), self.A(bk), 1.0,
                         T[tb].ap(), ALU.mult, ALU.add),
                     inc=[(self.rGem[bk].s, 1), (self.rT12[sl].s, 1),
                          (self.rW[sl].s, 1)],
                     waits=[("T", two)])
                self.rGem[bk].release()
                self.rT12[sl].release()
                self.rW[sl].release()
                if dc + 2 < NCH:
                    ldos.append(self.load_w8_sliced(
                        "wBo", 4352 + sl * 1024, 128,
                        ((dc + 2) * 128, (dc + 2) * 128 + 128), self.rW[sl]))
            P.op("tensor", lambda e: e.drain(), inc=self.rOp.s)
            self.rOp.release()
        P.barrier()

        self.stats_plain("R2row", "MR2row", EPS_2, srcfn=self.bch)
        P.barrier()
        for j in range(NT5):
            self.bcast(self.Bk(0), "R2row", j, 0, 512, waits=[self.rYb[0].wfree()])
            tM = self.bcast(self.Bk(1), "MR2row", j, 0, 512,
                            waits=[self.rYb[1].wfree()])
            for c in range(NCH):
                sl = c % 2
                tb = "t1b" if sl == 0 else "t2b"
                P.op("vector",
                     lambda e, c=c, j=j, tb=tb: e.tensor_tensor(
                         T[tb].ap(), self.bch(c, 512 * j, 512), self.Bk(0),
                         ALU.mult),
                     waits=([("T", tM)] if c == 0 else []) +
                     [self.rT12[sl].wfree()])
                vv = P.op("vector",
                          lambda e, tb=tb: e.tensor_tensor(
                              T[tb].ap(), T[tb].ap(), self.Bk(1), ALU.subtract),
                          inc=([("V", 1), (self.rYb[0].s, 1),
                                (self.rYb[1].s, 1)] if c == NCH - 1 else "V"))
                P.op("scalar",
                     lambda e, c=c, tb=tb: e.activation(
                         T["ybuf"].ap()[:, c * 512:(c + 1) * 512],
                         T[tb].ap(), AF.Identity,
                         bias=T["gcol"].ap()[:, 24 + c:25 + c],
                         scale=T["gcol"].ap()[:, 16 + c:17 + c]),
                     waits=[("V", vv)], inc=[("C", 1), (self.rT12[sl].s, 1)])
                self.rT12[sl].release()
            self.rYb[0].release()
            self.rYb[1].release()
            cy = P.tot["C"]
            for m in range(32):
                sl = m % 4
                ldm = self.load_w1("wB1", sl * 1024, m, self.rW[sl])
                bk = m % 2
                for c in range(NCH):
                    P.op("tensor",
                         lambda e, c=c, sl=sl, bk=bk: e.matmul(
                             self.A(bk), self.wsl(sl * 1024 + c * 128, 128),
                             T["ybuf"].ap()[:, c * 512:(c + 1) * 512],
                             start=(c == 0), stop=(c == NCH - 1)),
                         waits=([("LD", ldm), self.rGem[bk].wfree()] +
                                ([("C", cy)] if m == 0 else []))
                         if c == 0 else [],
                         inc="T" if c == NCH - 1 else None)
                th = P.tot["T"]
                P.op("scalar",
                     lambda e, m=m, bk=bk: e.activation(
                         T["abuf"].ap()[:, m * 512:(m + 1) * 512],
                         self.A(bk), AF.Relu,
                         bias=T["b1col"].ap()[:, m:m + 1], scale=1.0),
                     waits=[("T", th)],
                     inc=[(self.rGem[bk].s, 1), (self.rW[sl].s, 1)])
                self.rGem[bk].release()
                self.rW[sl].release()
            ch1 = P.op("scalar", lambda e: e.drain(), inc="C")
            P.wait("gpsimd", [self.rW[0].wfree(), self.rW[1].wfree(),
                              self.rW[2].wfree(), self.rW[3].wfree()])
            ldms = [self.load_w1("wB2", (d0 % 2) * 4096, d0,
                                 self.rW[d0 % 2]) for d0 in range(2)]
            for dc in range(NCH):
                sl = dc % 2
                ldm = ldms[dc]
                bk = 2 + dc % 2
                for m in range(32):
                    P.op("tensor",
                         lambda e, m=m, sl=sl, bk=bk: e.matmul(
                             self.A(bk), self.wsl(sl * 4096 + m * 128, 128),
                             T["abuf"].ap()[:, m * 512:(m + 1) * 512],
                             start=(m == 0), stop=(m == 31)),
                         waits=([("LD", ldm), self.rV[dc % 2].wfree()] +
                                ([("C", ch1)] if dc == 0 else []))
                         if m == 0 else [],
                         inc="T" if m == 31 else None)
                tm2 = P.tot["T"]
                sl2 = dc % 2
                tb = "t1b" if sl2 == 0 else "t2b"
                P.op("vector",
                     lambda e, dc=dc, j=j, bk=bk, tb=tb: e.scalar_tensor_tensor(
                         T[tb].ap(), self.A(bk),
                         T["pcol"].ap()[:, 12 + dc:13 + dc],
                         self.bch(dc, 512 * j, 512), ALU.add, ALU.add),
                     waits=[("T", tm2), self.rT12[sl2].wfree()],
                     inc=self.rV[dc % 2].s)
                self.rV[dc % 2].release()
                P.op("vector",
                     lambda e, dc=dc, j=j, tb=tb: e.tensor_tensor(
                         self.sch(dc, 512 * j, 512), T[tb].ap(),
                         self.sch(dc, 512 * j, 512), ALU.add),
                     inc=[(self.rT12[sl2].s, 1), (self.rW[sl].s, 1)])
                self.rT12[sl2].release()
                self.rW[sl].release()
                if dc + 2 < NCH:
                    ldms.append(self.load_w1("wB2", sl * 4096, dc + 2,
                                             self.rW[sl]))
        P.barrier()

    def _cluster_softmax(self, src, woff, g4, j, is_q):
        RB = (0, 64, 32, 96)
        P, T = self.P, self.T
        P.wait("tensor", [self.rKb.wfree(), self.rBc.wfree()])
        for hh in range(4):
            hq = 4 * g4 + hh
            pair = hq // 2
            bko = 1536 if hh % 2 == 0 else 1024
            P.op("tensor",
                 lambda e, hh=hh, hq=hq, pair=pair, woff=woff, bko=bko:
                 e.matmul(
                     self.psB.ap()[RB[hh]:RB[hh] + 32, bko:bko + 512],
                     self.wsl(woff + pair * 32, 32)
                     [64 * (hq % 2):64 * (hq % 2) + 64, :],
                     src(hh), start=True, stop=True,
                     tile_position=(64 * (hq % 2), RB[hh])),
                 inc="T" if hh == 3 else None)
        tlg = P.tot["T"]
        P.op("scalar",
             lambda e: e.activation(T["expkb"].ap()[0:64, :],
                                    self.psB.ap()[0:64, 1536:2048],
                                    AF.Exp, bias=0.0, scale=1.0),
             waits=[("T", tlg)])
        ce = P.op("scalar",
                  lambda e: e.activation(T["expkb"].ap()[64:128, :],
                                         self.psB.ap()[64:128, 1024:1536],
                                         AF.Exp, bias=0.0, scale=1.0),
                  inc="C")
        td = P.op("tensor",
                  lambda e: e.matmul(self.psB.ap()[0:4, 1536:2048],
                                     T["e4b16"].ap(), T["expkb"].ap(),
                                     start=True, stop=True),
                  waits=[("C", ce)], inc="T")
        vr = P.op("vector",
                  lambda e: e.reciprocal(T["rowsC"].ap()[64:68, :],
                                         self.psB.ap()[0:4, 1536:2048]),
                  waits=[("T", td), self.rRowA.wfree()], inc="V")
        vb = P.op("vector",
                  lambda e: e.tensor_copy(T["hRb"].ap()[0:4, :],
                                          T["rowsC"].ap()[64:68, :]),
                  waits=[("V", vr)], inc=[("V", 1), (self.rRowA.s, 1)])
        self.rRowA.release()
        tb = P.op("tensor",
                  lambda e: e.matmul(self.psB.ap()[:, 1536:2048],
                                     T["e4bb"].ap(), T["hRb"].ap()[0:4, :],
                                     start=True, stop=True),
                  waits=[("V", vb)], inc="T")
        if is_q:
            P.op("vector",
                 lambda e, g4=g4: e.tensor_tensor(
                     T["qbbuf"].ap()[:, g4 * 512:(g4 + 1) * 512],
                     T["expkb"].ap(), self.psB.ap()[:, 1536:2048], ALU.mult),
                 waits=[("T", tb)],
                 inc=[(self.rKb.s, 1), (self.rBc.s, 1)])
        else:
            P.op("vector",
                 lambda e: e.tensor_tensor(
                     T["expkb"].ap(), T["expkb"].ap(),
                     self.psB.ap()[:, 1536:2048], ALU.mult),
                 waits=[("T", tb)], inc=self.rBc.s)
            P.op("vector",
                 lambda e, g4=g4, j=j: e.tensor_scalar(
                     T["kbbuf"].ap()[:, g4 * S + 512 * j:
                                     g4 * S + 512 * j + 512],
                     T["expkb"].ap(), 0.1, None, ALU.mult),
                 inc=self.rKb.s)
        self.rKb.release()
        self.rBc.release()

    def emit(self):
        self.prologue()
        P = self.P
        for bi in range(BC):
            for c in range(NCH):
                self.dma_g(lambda c=c, bi=bi: self.sch(c, 0, S),
                           lambda c=c, bi=bi: self.dram["xfeat"][bi, c])
            ldx = P.tot["LD"]
            for e in ("tensor", "vector", "scalar"):
                P.wait(e, [("LD", ldx)])
            P.barrier()
            ns = self.nsub
            if ns >= 1:
                self.attn_a(bi, final=(ns == 1))
            if ns >= 2:
                self.ffn(bi, final=(ns == 2))
            if ns >= 3:
                self.attn_b(bi, final=(ns == 3))
            if ns >= 4:
                self.ffn(bi, final=True)
            if ns in (0, 3):
                for c in range(NCH):
                    for j in range(NT5):
                        osl = 0
                        vv = P.op("vector",
                                  lambda e, c=c, j=j, osl=osl: e.tensor_copy(
                                      self.T["outsb"].ap()
                                      [:, osl * 512:osl * 512 + 512],
                                      self.sch(c, 512 * j, 512)),
                                  waits=[self.rOut[osl].wfree()], inc="V")
                        P.wait("sync", [("V", vv)])
                        self.dma_s(lambda c=c, j=j, bi=bi:
                                   self.yout[bi, c][:, 512 * j:512 * j + 512],
                                   lambda osl=osl: self.T["outsb"].ap()
                                   [:, osl * 512:osl * 512 + 512], inc="YD")
                        P.op("sync", lambda e: e.drain(),
                             waits=[("YD", P.tot["YD"])], inc=self.rOut[osl].s)
                        self.rOut[osl].release()
            P.barrier()
        P.wait("sync", [("YD", P.tot.get("YD", 0))])


def build_program(nsub=NSUB):
    nc = bass.Bass("TRN2", target_bir_lowering=False,
                   detect_race_conditions=False)
    with ExitStack() as stack:
        K = Kern(nc, stack, nsub)
        K.emit()
        block = stack.enter_context(nc.Block())
        K.P.finalize(block)
    return nc


def kernel(**inputs):
    w = _host_prepare(inputs)
    x = np.asarray(inputs["L0_attn_a_x"], np.float32)
    nc = build_program(NSUB)
    in_maps = []
    for core in range(8):
        xs = x[core * BC:(core + 1) * BC]
        xf = np.ascontiguousarray(xs.transpose(0, 2, 1)).reshape(BC, 8, 128, S)
        m = {"xfeat": xf}
        m.update(w)
        in_maps.append(m)
    res = run_bass_kernel_spmd(nc, in_maps, core_ids=list(range(8)))
    outs = []
    for core in range(8):
        yf = np.asarray(res.results[core]["yout"])
        outs.append(yf.reshape(BC, 1024, S).transpose(0, 2, 1))
    return np.concatenate(outs, 0).astype(np.float32)


# version 14
# speedup vs baseline: 1.0005x; 1.0005x over previous
import numpy as np
from contextlib import ExitStack

import concourse.bass as bass
import concourse.mybir as mybir
from concourse.bass_utils import run_bass_kernel_spmd

F32 = mybir.dt.float32
BF16 = mybir.dt.bfloat16
AF = mybir.ActivationFunctionType
ALU = mybir.AluOpType

S = 2048
BC = 4
NCH = 8
NT5 = 4
NT1 = 16
C_A, C_F, C_B = 12.0, 14.0, 8.0
EPS_P, EPS_H, EPS_2 = 1e-5, 1e-5, 1e-6
NSUB = 4
DBG = None
DBG_NJT = None
DBG_QS = None
DBG_BUF = "abuf"

ENGS = ("sync", "gpsimd", "tensor", "scalar", "vector")


class Prog:
    def __init__(self, nc, stack):
        self.nc = nc
        self.stack = stack
        self.steps = {e: [] for e in ENGS}
        self.sems = {}
        self.tot = {}
        self.nbar = 0

    def sem(self, name):
        if name not in self.sems:
            self.sems[name] = self.stack.enter_context(self.nc.semaphore(name))
            self.tot[name] = 0
        return name

    def op(self, eng, fn, waits=(), inc=None, incby=1):
        if inc is None:
            incs = ()
        elif isinstance(inc, str):
            incs = ((inc, incby),)
        else:
            incs = tuple(inc)
        post = None
        for s, by in incs:
            self.sem(s)
            self.tot[s] += by
            if post is None:
                post = self.tot[s]
        for s, _ in waits:
            self.sem(s)
        sems = self.sems
        w = tuple((sems[s], int(v)) for s, v in waits if v > 0)

        def thunk(e, fn=fn, w=w, incs=incs):
            for s, v in w:
                e.wait_ge(s, v)
            r = fn(e)
            if incs:
                r.then_inc(sems[incs[0][0]], incs[0][1])
                for s, by in incs[1:]:
                    try:
                        e.drain().then_inc(sems[s], by)
                    except Exception:
                        e.nop().then_inc(sems[s], by)
        self.steps[eng].append(thunk)
        return post

    def wait(self, eng, waits):
        for s, _ in waits:
            self.sem(s)
        waits = [(s, v) for s, v in waits]
        sems = self.sems
        w = tuple((sems[s], int(v)) for s, v in waits if v > 0)

        def thunk(e, w=w):
            for s, v in w:
                e.wait_ge(s, v)
        self.steps[eng].append(thunk)

    def barrier(self):
        ph = self.sem("phase")
        self.nbar += 1
        tgt = 5 * self.nbar
        sems = self.sems
        for e in ENGS:
            def thunk(en, sems=sems, ph=ph):
                try:
                    en.drain()
                except Exception:
                    pass
                en.sem_inc(sems[ph], 1)
            self.steps[e].append(thunk)
        self.tot[ph] = tgt
        for e in ENGS:
            self.wait(e, [(ph, tgt)])

    def finalize(self, block):
        for e in ENGS:
            def runner(eng, e=e):
                for f in self.steps[e]:
                    f(eng)
            getattr(block, e)(runner)


class Res:
    _n = 0

    def __init__(self, P):
        Res._n += 1
        self.P = P
        self.s = P.sem("r%d" % Res._n)
        self.val = 0

    def wfree(self):
        return (self.s, self.val)

    def release(self):
        self.val = self.P.tot[self.s]


def _host_prepare(inp):
    f = np.float32
    c = np.ascontiguousarray
    w = {}
    qkv = np.asarray(inp["L0_attn_a_qkv_w"], f)
    wq = qkv[0:1024].T
    wk = qkv[1024:2048].T
    w["wAqk"] = c(np.concatenate([wq, wk], 1).reshape(8, 128, 2048)
                  .reshape(8, 128, 8, 256).transpose(2, 0, 1, 3))
    w["wAv"] = c(qkv[2048:3072].T.reshape(8, 128, 1024)
                 .reshape(8, 128, 2, 512).transpose(2, 0, 1, 3))
    w["wAproj"] = c(np.asarray(inp["L0_attn_a_proj_w"], f).T.reshape(8, 128, 1024))
    w["sinks"] = c(np.asarray(inp["L0_attn_a_sinks"], f).reshape(1, 16))
    w["wFq"] = c(np.asarray(inp["L0_ffn_Wq"], f).reshape(8, 128, 512))
    w["wFkv"] = c(np.asarray(inp["L0_ffn_Wkv"], f).reshape(8, 128, 1024))
    w["wFs"] = c(np.asarray(inp["L0_ffn_Ws"], f).reshape(8, 128, 8))
    w["wFo"] = c(np.asarray(inp["L0_ffn_Wo"], f).reshape(512, 1024).reshape(4, 128, 1024))
    w["boF"] = c(np.asarray(inp["L0_ffn_bo"], f))
    w["wBq"] = c(np.asarray(inp["L0_attn_b_Wq"], f).reshape(1024, 1024)
                 .reshape(8, 128, 4, 256).transpose(2, 0, 1, 3))
    w["wBk"] = c(np.asarray(inp["L0_attn_b_Wk"], f).reshape(1024, 1024).reshape(8, 128, 1024))
    w["wBv"] = c(np.asarray(inp["L0_attn_b_Wv"], f).reshape(1024, 1024).reshape(8, 128, 1024))
    whq = np.asarray(inp["L0_attn_b_Whq"], f)
    whk = np.asarray(inp["L0_attn_b_Whk"], f)
    w["wBhq"] = c(np.stack([np.concatenate([whq[2 * i], whq[2 * i + 1]], 0)
                            for i in range(8)], 1).reshape(128, 256))
    w["wBhk"] = c(np.stack([np.concatenate([whk[2 * i], whk[2 * i + 1]], 0)
                            for i in range(8)], 1).reshape(128, 256))
    w["wBo"] = c(np.asarray(inp["L0_attn_b_Wo"], f).reshape(1024, 1024)
                 .reshape(8, 128, 8, 128).transpose(2, 0, 1, 3))
    w1 = np.asarray(inp["L0_attn_b_W1"], f).reshape(8, 128, 32, 128)
    w["wB1"] = c(w1.transpose(2, 1, 0, 3).reshape(32, 128, 1024))
    w2 = np.asarray(inp["L0_attn_b_W2"], f).reshape(32, 128, 8, 128)
    w["wB2"] = c(w2.transpose(2, 1, 0, 3).reshape(8, 128, 4096))
    pc = np.zeros((128, 28), f)
    pc[:, 0:8] = np.asarray(inp["L0_attn_a_proj_b"], f).reshape(8, 128).T
    pc[:, 8] = np.tile(np.asarray(inp["L0_attn_a_qn_w"], f), 2)
    pc[:, 9] = np.tile(np.asarray(inp["L0_attn_a_qn_b"], f), 2)
    pc[:, 10] = np.tile(np.asarray(inp["L0_attn_a_kn_w"], f), 2)
    pc[:, 11] = np.tile(np.asarray(inp["L0_attn_a_kn_b"], f), 2)
    pc[:, 12:20] = np.asarray(inp["L0_attn_b_b2"], f).reshape(8, 128).T
    w["pcol"] = c(pc)
    gc = np.zeros((128, 32), f)
    gc[:, 0:8] = np.asarray(inp["L0_attn_b_ln1_g"], f).reshape(8, 128).T
    gc[:, 8:16] = np.asarray(inp["L0_attn_b_ln1_b"], f).reshape(8, 128).T
    gc[:, 16:24] = np.asarray(inp["L0_attn_b_ln2_g"], f).reshape(8, 128).T
    gc[:, 24:32] = np.asarray(inp["L0_attn_b_ln2_b"], f).reshape(8, 128).T
    w["gcol"] = c(gc)
    w["b1"] = c(np.asarray(inp["L0_attn_b_b1"], f).reshape(32, 128).T)

    pos = np.arange(S, dtype=np.float64)
    invf = 1.0 / (10000.0 ** (np.arange(0, 64, 2, dtype=np.float64) / 64.0))
    fr = pos[:, None] * invf[None, :]
    emb = np.concatenate([fr, fr], 1)
    cos64 = np.cos(emb).T.astype(f)
    sin64 = np.sin(emb).T.astype(f)
    w["cos128"] = c(np.concatenate([cos64, cos64], 0))
    w["sin128"] = c(np.concatenate([sin64, sin64], 0))
    kp = np.arange(128)[:, None]
    qf = np.arange(128)[None, :]
    rotP = np.zeros((128, 128), f)
    for m in range(128):
        if m % 64 < 32:
            rotP[m + 32, m] = -1.0
        else:
            rotP[m - 32, m] = 1.0
    w["rotP"] = rotP
    w["maskU"] = c((kp > qf).astype(f))
    w["maskL"] = c((kp <= qf).astype(f))
    w["ones128"] = c(np.ones((128, 1), f))
    w["onesrow"] = c(np.ones((1, 128), f))
    w["identb"] = c(np.eye(128, dtype=f))
    e16 = np.zeros((8, 128, 16), f)
    for ch in range(8):
        e16[ch, 0:64, 2 * ch] = 1.0
        e16[ch, 64:128, 2 * ch + 1] = 1.0
    w["e16"] = c(e16)
    eb1 = np.zeros((8, 16, 128), f)
    for ch in range(8):
        eb1[ch, 2 * ch, 0:64] = 1.0
        eb1[ch, 2 * ch + 1, 64:128] = 1.0
    w["eb1"] = c(eb1)
    eb2 = np.zeros((33, 128), f)
    eb2[0, 0:64] = 1.0
    eb2[32, 64:128] = 1.0
    w["eb2"] = c(eb2)
    e4 = np.zeros((128, 4), f)
    e4b = np.zeros((4, 128), f)
    for i in range(4):
        e4[32 * i:32 * i + 32, i] = 1.0
        e4b[i, 32 * i:32 * i + 32] = 1.0
    w["e4"] = c(e4)
    w["e4b"] = c(e4b)
    import ml_dtypes
    for n in BF_W:
        w[n] = np.ascontiguousarray(
            w[n].astype(ml_dtypes.bfloat16)).view(np.float32)
    return w


BF_W = {"wAproj", "wFq", "wFkv", "wFo", "wBk", "wBv", "wAqk", "wAv",
        "wBq", "wBo", "wB1", "wB2", "wBhq", "wBhk", "cos128", "sin128"}

WSPEC = [
    ("wAqk", (8, 8, 128, 256)), ("wAv", (2, 8, 128, 512)),
    ("wAproj", (8, 128, 1024)),
    ("sinks", (1, 16)), ("wFq", (8, 128, 512)), ("wFkv", (8, 128, 1024)),
    ("wFs", (8, 128, 8)), ("wFo", (4, 128, 1024)), ("boF", (8, 1024)),
    ("wBq", (4, 8, 128, 256)), ("wBk", (8, 128, 1024)), ("wBv", (8, 128, 1024)),
    ("wBhq", (128, 256)), ("wBhk", (128, 256)), ("wBo", (8, 8, 128, 128)),
    ("wB1", (32, 128, 1024)), ("wB2", (8, 128, 4096)),
    ("pcol", (128, 28)), ("gcol", (128, 32)), ("b1", (128, 32)),
    ("cos128", (128, S)), ("sin128", (128, S)),
    ("maskU", (128, 128)), ("maskL", (128, 128)), ("rotP", (128, 128)),
    ("ones128", (128, 1)), ("onesrow", (1, 128)), ("identb", (128, 128)),
    ("e16", (8, 128, 16)), ("eb1", (8, 16, 128)), ("e4", (128, 4)), ("e4b", (4, 128)), ("eb2", (33, 128)),
]


class Kern:
    def __init__(self, nc, stack, nsub):
        self.nc = nc
        self.nsub = nsub
        self.P = Prog(nc, stack)
        self.E = stack.enter_context
        self.dram = {}
        self.T = {}
        self.dram["xfeat"] = nc.dram_tensor("xfeat", [BC, 8, 128, S], F32,
                                            kind="ExternalInput").ap()
        for name, shape in WSPEC:
            sh = list(shape)
            if name in BF_W:
                sh[-1] //= 2
            self.dram[name] = nc.dram_tensor(name, sh, F32,
                                             kind="ExternalInput").ap()
        self.yout = nc.dram_tensor("yout", [BC, 8, 128, S], F32,
                                   kind="ExternalOutput").ap()
        self._alloc()
        P = self.P
        self.rStat = [Res(P), Res(P)]
        self.rGem = [Res(P), Res(P)]
        self.rV = [Res(P), Res(P)]
        self.rYb = [Res(P), Res(P)]
        self.rQS = [Res(P), Res(P)]
        self.rQB = [Res(P), Res(P)]
        self.rSc = [Res(P), Res(P)]
        self.rOT = Res(P)
        self._epi_pend = None
        self.rBc = Res(P)
        self.rGt = Res(P)
        self.rPr = [Res(P), Res(P)]
        self.rYt = Res(P)
        self.rQt = Res(P)
        self.rOp = Res(P)
        self.rW = [Res(P), Res(P), Res(P), Res(P)]
        self.rOut = [Res(P), Res(P)]
        self.rXq = Res(P)
        self.rT12 = [Res(P), Res(P)]
        self.rRowA = Res(P)
        self.rQbb = Res(P)
        self.rKb = Res(P)

    def _alloc(self):
        nc, E, T = self.nc, self.E, self.T

        def sb(name, shape, dt):
            T[name] = E(nc.sbuf_tensor("s_" + name, list(shape), dt))

        sb("stream", (128, 8 * S), BF16)
        sb("abuf", (128, 8 * S), BF16)
        sb("bbuf", (128, 8 * S), BF16)
        sb("vbuf", (128, NT1 * 16 * 65), BF16)
        sb("kbbuf", (128, 4 * S), BF16)
        sb("wslab", (128, 8192), BF16)
        sb("probs", (128, 2048), BF16)
        sb("ybuf", (128, 8 * 512), BF16)
        sb("qbuf", (128, 8 * 512), BF16)
        sb("outp", (128, 8 * 512), BF16)
        sb("qbbuf", (128, 4 * 512), BF16)
        sb("expkb", (128, 512), BF16)
        sb("e4bb", (4, 128), BF16)
        sb("maskU", (128, 128), BF16)
        sb("maskL", (128, 128), BF16)
        sb("identb", (128, 128), BF16)
        sb("e16b", (128, 8 * 16), BF16)
        sb("e4b16", (128, 4), BF16)
        sb("t1b", (128, 512), BF16)
        sb("t2b", (128, 512), BF16)
        sb("Rrow", (97, 512), BF16)
        sb("MRrow", (97, 512), BF16)
        sb("Arow", (97, 512), BF16)
        sb("MArow", (97, 512), BF16)
        sb("hRb", (48, 512), BF16)
        sb("rowsA", (97, 512), BF16)
        sb("rowsB", (97, 512), BF16)
        sb("rowsC", (128, 512), F32)
        sb("gtmp", (128, 32), BF16)
        sb("gm8", (128, 8), F32)
        sb("gs", (128, 8), F32)
        sb("pcol", (128, 28), F32)
        sb("gcol", (128, 32), F32)
        sb("b1col", (128, 32), F32)
        sb("sinkrow", (1, 16), F32)
        sb("ones1b", (128, 1), BF16)
        sb("cbias", (128, 8), F32)
        sb("onesrf", (97, 64), F32)
        sb("eb2f", (33, 128), F32)
        sb("onesrb", (97, 128), BF16)
        class _View:
            def __init__(self, t, p0, p1, c0=None, c1=None):
                self.t, self.p0, self.p1 = t, p0, p1
                self.c0, self.c1 = c0, c1

            def ap(self):
                a = self.t.ap()[self.p0:self.p1, :]
                if self.c0 is not None:
                    a = a[:, self.c0:self.c1]
                return a
        class _ViewC:
            def __init__(self, t, p0, p1, c0, c1, dt=None):
                self.t, self.p0, self.p1 = t, p0, p1
                self.c0, self.c1, self.dt = c0, c1, dt

            def ap(self):
                a = self.t.ap()[self.p0:self.p1, self.c0:self.c1]
                if self.dt is not None:
                    a = a.bitcast(self.dt)
                return a
        T["outsb"] = _ViewC(T["ybuf"], 0, 128, 1536, 2560, F32)
        T["gtr"] = _ViewC(T["ybuf"], 0, 4, 2560, 2688)
        T["boFb"] = _ViewC(T["ybuf"], 0, 8, 2816, 3840)
        T["eb1b"] = _ViewC(T["ybuf"], 0, 48, 2048 + 1024, 2048 + 2048)
        T["zt"] = _View(T["ybuf"], 0, 128, 0, 512)
        T["ropA"] = _View(T["ybuf"], 0, 128, 512, 1024)
        T["xsq"] = _View(T["ybuf"], 0, 128, 1024, 1536)
        T["hSm"] = _View(T["rowsA"], 0, 16)
        T["hR"] = _View(T["rowsB"], 0, 16)
        T["R2row"] = T["Arow"]
        T["MR2row"] = T["MArow"]
        self.psA = E(nc.psum_tensor("psA", [128, 2048], F32))
        self.psB = E(nc.psum_tensor("psB", [128, 2048], F32))

    def A(self, b, n=1):
        return self.psA.ap()[:, b * 512:(b + n) * 512]

    def Bk(self, b, n=1):
        return self.psB.ap()[:, b * 512:(b + n) * 512]

    def sch(self, c, t0, n):
        return self.T["stream"].ap()[:, c * S + t0:c * S + t0 + n]

    def ach(self, c, t0, n):
        return self.T["abuf"].ap()[:, c * S + t0:c * S + t0 + n]

    def bch(self, c, t0, n):
        return self.T["bbuf"].ap()[:, c * S + t0:c * S + t0 + n]

    def kch(self, c, t0, n):
        return self.T["kbbuf"].ap()[:, c * S + t0:c * S + t0 + n]

    def wsl(self, off, ncols):
        return self.T["wslab"].ap()[:, off:off + ncols]

    def dma_g(self, dstfn, srcfn, inc="LD"):
        return self.P.op("gpsimd", lambda e: e.dma_start(out=dstfn(), in_=srcfn()),
                         inc=inc, incby=16)

    def dma_s(self, dstfn, srcfn, inc="WL"):
        return self.P.op("sync", lambda e: e.dma_start(out=dstfn(), in_=srcfn()),
                         inc=inc, incby=16)

    def wdram(self, name):
        sh = dict(WSPEC)[name]
        if len(sh) == 3:
            return self.dram[name]
        return self.dram[name]

    def load_w8(self, name, off, cols, nch=8):
        bw = name in BF_W
        for c in range(nch):
            src = self.wdram(name)[c]
            self.dma_g(lambda c=c, off=off, cols=cols, bw=bw:
                       self.wsl(off + c * cols, cols).bitcast(F32)
                       if bw else self.wsl(off + c * cols, cols),
                       lambda src=src: src)
        return self.P.tot["LD"]

    def load_w8_sliced(self, name, off, cols, csl, res):
        self.P.wait("gpsimd", [res.wfree()])
        idx = csl[0] // cols
        for c in range(NCH):
            src = self.dram[name][idx, c]
            self.dma_g(lambda c=c, off=off, cols=cols:
                       self.wsl(off + c * cols, cols).bitcast(F32),
                       lambda src=src: src)
        return self.P.tot["LD"]

    def load_w1(self, name, off, m, res):
        self.P.wait("gpsimd", [res.wfree()])
        bw = name in BF_W
        sh = dict(WSPEC)[name]
        src = self.dram[name][m]
        self.dma_g(lambda off=off, n=sh[2], bw=bw:
                   self.wsl(off, n).bitcast(F32) if bw else self.wsl(off, n),
                   lambda src=src: src)
        return self.P.tot["LD"]

    def prologue(self):
        P, T, dram = self.P, self.T, self.dram
        ds = self.dma_s
        self.dma_g(lambda: T["maskU"].ap(), lambda: dram["maskU"])
        self.dma_g(lambda: T["maskL"].ap(), lambda: dram["maskL"])
        self.dma_g(lambda: T["identb"].ap(), lambda: dram["identb"])
        P.op("vector", lambda e: e.memset(T["ones1b"].ap(), 1.0), inc="V")
        P.op("vector", lambda e: e.memset(T["onesrf"].ap(), 1.0), inc="V")
        P.op("vector", lambda e: e.memset(T["rowsC"].ap(), 0.0), inc="V")
        ds(lambda: T["eb2f"].ap(), lambda: dram["eb2"])
        P.op("vector", lambda e: e.memset(T["onesrb"].ap(), 1.0), inc="V")
        self.dma_g(lambda: T["e16b"].ap().rearrange("p (c n) -> p c n", n=16),
                   lambda: dram["e16"].transpose([1, 0, 2]))
        self.dma_g(lambda: T["e4b16"].ap(), lambda: dram["e4"])
        self.dma_g(lambda: T["e4bb"].ap(), lambda: dram["e4b"])
        ds(lambda: T["pcol"].ap(), lambda: dram["pcol"])
        ds(lambda: T["gcol"].ap(), lambda: dram["gcol"])
        ds(lambda: T["b1col"].ap(), lambda: dram["b1"])
        ds(lambda: T["sinkrow"].ap(), lambda: dram["sinks"])
        import math as _m
        for i, v in enumerate([-C_A, -C_F, -C_B, 1024.0 * 1024.0 * EPS_P,
                               _m.log(1024.0), 4096.0 * EPS_H, _m.log(64.0),
                               1024.0 * 1024.0 * EPS_2]):
            P.op("vector", lambda e, i=i, v=v: e.memset(
                T["cbias"].ap()[:, i:i + 1], v), inc="V")
        wl = P.tot["WL"]
        P.op("scalar", lambda e: e.activation(T["sinkrow"].ap(), T["sinkrow"].ap(),
                                              AF.Exp, bias=T["cbias"].ap()[0:1, 0:1], scale=1.0),
             waits=[("WL", wl), ("LD", P.tot["LD"]), ("V", P.tot["V"])], inc="C")
        P.barrier()

    def stats_plain(self, Rdst, MRdst, eps, srcfn=None, hold_rows=False):
        P, T = self.P, self.T
        srcfn = srcfn or self.sch
        P.wait("tensor", [self.rStat[0].wfree(), self.rStat[1].wfree(),
                          self.rRowA.wfree()])
        P.wait("vector", [self.rStat[0].wfree(), self.rStat[1].wfree()])
        vms = P.op("vector", lambda e: e.memset(
            self.psA.ap()[:, 0:1024], 0.0), inc="V")
        tgrp = 0
        for j in range(NT5):
            for c in range(NCH):
                w = []
                if j == 0 and c == 0:
                    w.append(("V", vms))
                if j > 0 and c == 0:
                    w.append(("T", tgrp))
                P.op("tensor",
                     lambda e, j=j, c=c: e.matmul(
                         self.psA.ap()[32 * j:32 * j + 1, 0:512],
                         T["ones1b"].ap(), srcfn(c, 512 * j, 512),
                         start=(c == 0), stop=(c == NCH - 1),
                         tile_position=(0, 32 * j)),
                     waits=w,
                     inc="T" if c == NCH - 1 else None)
            tgrp = P.tot["T"]
        for j in range(NT5):
            for c in range(NCH):
                vsq = P.op("vector",
                           lambda e, j=j, c=c: e.tensor_tensor(
                               T["xsq"].ap(), srcfn(c, 512 * j, 512),
                               srcfn(c, 512 * j, 512), ALU.mult),
                           waits=[self.rXq.wfree()], inc="V")
                w = [("V", vsq)]
                if j > 0 and c == 0:
                    w.append(("T", tgrp))
                P.op("tensor",
                     lambda e, j=j, c=c: e.matmul(
                         self.psA.ap()[32 * j:32 * j + 1, 512:1024],
                         T["ones1b"].ap(), T["xsq"].ap(),
                         start=(c == 0), stop=(c == NCH - 1),
                         tile_position=(0, 32 * j)),
                     waits=w,
                     inc=[(self.rXq.s, 1), ("T", 1)] if c == NCH - 1
                     else self.rXq.s)
                self.rXq.release()
            tgrp = P.tot["T"]
        tmm = P.op("tensor", lambda e: e.drain(), inc="T")
        P.op("vector", lambda e: e.tensor_copy(T["rowsB"].ap(),
                                               self.psA.ap()[0:97, 0:512]),
             waits=[("T", tmm)], inc=self.rStat[0].s)
        self.rStat[0].release()
        P.op("vector", lambda e: e.tensor_tensor(T["rowsA"].ap(), T["rowsB"].ap(),
                                                 T["rowsB"].ap(), ALU.mult))
        if DBG == "a_stats":
            P.op("vector", lambda e: e.tensor_copy(
                T["outp"].ap()[0:97, 512:1024], T["rowsA"].ap()))
        P.op("vector", lambda e: e.scalar_tensor_tensor(
            T["rowsA"].ap(), self.psA.ap()[0:97, 512:1024], 1024.0,
            T["rowsA"].ap(), ALU.mult, ALU.subtract), inc=self.rStat[1].s)
        self.rStat[1].release()
        if DBG == "a_stats":
            P.op("vector", lambda e: e.tensor_copy(
                T["outp"].ap()[0:97, 1024:1536], T["rowsA"].ap()))
        vr = P.op("vector", lambda e: e.drain(), inc="V")
        ecol = 3 if eps == EPS_P else 7
        P.op("scalar", lambda e, ecol=ecol: e.activation(
            T["rowsA"].ap(), T["rowsA"].ap(), AF.Ln,
            bias=self.T["cbias"].ap()[0:97, ecol:ecol + 1], scale=1.0),
            waits=[("V", vr)])
        if DBG == "a_stats":
            P.op("scalar", lambda e: e.copy(
                T["outp"].ap()[0:97, 1536:2048], T["rowsA"].ap()))
        cr = P.op("scalar", lambda e: e.activation(
            T["rowsA"].ap(), T["rowsA"].ap(), AF.Exp,
            bias=self.T["cbias"].ap()[0:97, 4:5], scale=-0.5), inc="C")
        P.op("vector", lambda e, Rdst=Rdst: e.tensor_copy(T[Rdst].ap(),
                                                          T["rowsA"].ap()),
             waits=[("C", cr)])
        incs = [("V", 1)] if hold_rows else [("V", 1), (self.rRowA.s, 1)]
        P.op("vector", lambda e, MRdst=MRdst: e.scalar_tensor_tensor(
            T[MRdst].ap(), T["rowsB"].ap(), 1.0 / 1024.0, T["rowsA"].ap(),
            ALU.mult, ALU.mult), inc=incs)
        if not hold_rows:
            self.rRowA.release()
        return P.tot["V"]

    def bcast(self, dst, rname, j, q0, n, waits=(), tpcol=0, incs=None):
        return self.P.op("tensor",
                         lambda e: e.matmul(
                             dst, self.T["onesrb"].ap()[32 * j:32 * j + 1, :],
                             self.T[rname].ap()[32 * j:32 * j + 1, q0:q0 + n],
                             start=True, stop=True,
                             tile_position=(32 * j, tpcol)),
                         waits=waits, inc=incs or "T")

    def ytile(self, j, Rname, MRname, dstfn, vwait=0, srcfn=None):
        P, T = self.P, self.T
        srcfn = srcfn or self.sch
        if DBG == "noln":
            for c in range(NCH):
                P.op("vector",
                     lambda e, c=c: e.tensor_copy(dstfn(c), srcfn(c, 512 * j, 512)),
                     inc="V" if c == NCH - 1 else None)
            return P.tot["V"]
        w = [self.rYb[0].wfree()]
        if vwait:
            w.append(("V", vwait))
        self.bcast(self.Bk(0), Rname, j, 0, 512, waits=w)
        tM = self.bcast(self.Bk(1), MRname, j, 0, 512,
                        waits=[self.rYb[1].wfree()])
        for c in range(NCH):
            sl = c % 2
            tb = "t1b" if sl == 0 else "t2b"
            P.op("vector",
                 lambda e, c=c, tb=tb: e.tensor_tensor(
                     T[tb].ap(), srcfn(c, 512 * j, 512), self.Bk(0), ALU.mult),
                 waits=([("T", tM)] if c == 0 else []) + [self.rT12[sl].wfree()],
                 inc=([(self.rYb[0].s, 1)] if c == NCH - 1 else None))
            P.op("vector",
                 lambda e, c=c, tb=tb: e.tensor_tensor(
                     dstfn(c), T[tb].ap(), self.Bk(1), ALU.subtract),
                 inc=([(self.rYb[1].s, 1), (self.rT12[sl].s, 1), ("V", 1)]
                      if c == NCH - 1 else [(self.rT12[sl].s, 1)]))
            self.rT12[sl].release()
        self.rYb[0].release()
        self.rYb[1].release()
        return P.tot["V"]

    def y1tile(self, j, dstfn):
        P, T = self.P, self.T
        self.bcast(self.Bk(0), "Arow", j, 0, 512, waits=[self.rYb[0].wfree()])
        tM = self.bcast(self.Bk(1), "MArow", j, 0, 512,
                        waits=[self.rYb[1].wfree()])
        for c in range(NCH):
            sl = c % 2
            tb = "t1b" if sl == 0 else "t2b"
            P.op("vector",
                 lambda e, c=c, tb=tb: e.tensor_tensor(
                     T[tb].ap(), self.sch(c, 512 * j, 512), self.Bk(0), ALU.mult),
                 waits=([("T", tM)] if c == 0 else []) + [self.rT12[sl].wfree()])
            vv = P.op("vector",
                      lambda e, c=c, tb=tb: e.tensor_tensor(
                          T["t3b"].ap() if False else T[tb].ap(),
                          T[tb].ap(), self.Bk(1), ALU.subtract),
                      inc=([("V", 1), (self.rYb[0].s, 1), (self.rYb[1].s, 1)]
                           if c == NCH - 1 else "V"))
            P.op("scalar",
                 lambda e, c=c, tb=tb: e.activation(
                     dstfn(c), T[tb].ap(), AF.Identity,
                     bias=T["gcol"].ap()[:, 8 + c:9 + c],
                     scale=T["gcol"].ap()[:, c:c + 1]),
                 waits=[("V", vv)], inc=[("C", 1), (self.rT12[sl].s, 1)])
            self.rT12[sl].release()
        self.rYb[0].release()
        self.rYb[1].release()
        return P.tot["C"]

    def attn_a(self, bi, final):
        P, T = self.P, self.T
        self.stats_plain("Rrow", "MRrow", EPS_P)
        P.barrier()
        if DBG == "a_stats":
            P.op("vector", lambda e: e.tensor_copy(
                T["outp"].ap()[0:97, 0:512], self.psA.ap()[0:97, 512:1024]),
                inc="V")
            P.wait("gpsimd", [("V", P.tot["V"])])
            if bi == 0:
                self.dma_g(lambda: self.yout[0, 4][0:97, 0:512],
                           lambda: T["outp"].ap()[0:97, 0:512], inc="YD")
                self.dma_g(lambda: self.yout[0, 5][0:97, 0:512],
                           lambda: T["outp"].ap()[0:97, 512:1024], inc="YD")
                self.dma_g(lambda: self.yout[0, 6][0:97, 0:512],
                           lambda: T["outp"].ap()[0:97, 1024:1536], inc="YD")
                self.dma_g(lambda: self.yout[0, 7][0:97, 0:512],
                           lambda: T["outp"].ap()[0:97, 1536:2048], inc="YD")
                self.dma_g(lambda: self.yout[0, 0][0:97, 0:512],
                           lambda: T["Rrow"].ap(), inc="YD")
                self.dma_g(lambda: self.yout[0, 1][0:97, 0:512],
                           lambda: T["MRrow"].ap(), inc="YD")
                self.dma_g(lambda: self.yout[0, 2][0:97, 0:512],
                           lambda: T["rowsA"].ap(), inc="YD")
                self.dma_g(lambda: self.yout[0, 3][0:97, 0:512],
                           lambda: T["rowsB"].ap(), inc="YD")
            P.wait("gpsimd", [("YD", P.tot.get("YD", 0))])
            P.barrier()
            return

        if bi == 0:
            P.op("vector", lambda e: e.memset(
                T["vbuf"].ap().rearrange("p (a c) -> p a c", c=65)[:, :, 64:65],
                1.0), inc="V")

        ymap = [("qbuf", 0), ("qbuf", 2048), ("outp", 0), ("outp", 2048),
                ("probs", 0), ("probs", 512), ("probs", 1024), ("probs", 1536)]

        def yfull(c, off=0, n=512):
            tn, base = ymap[c]
            return self.T[tn].ap()[:, base + off: base + off + n]

        wslot = [0, 2048]
        for j in range(NT5):
            yv = self.ytile(j, "Rrow", "MRrow", lambda c: yfull(c))
            for op_ in range(8):
                sl = op_ % 2
                ld = self.load_w8_sliced("wAqk", wslot[sl], 256,
                                         (op_ * 256, op_ * 256 + 256), self.rW[sl])
                for oc2 in range(2):
                    oc = 2 * op_ + oc2
                    bk = oc % 2
                    for c in range(NCH):
                        P.op("tensor",
                             lambda e, c=c, oc2=oc2, sl=sl, bk=bk: e.matmul(
                                 self.A(bk),
                                 self.wsl(wslot[sl] + c * 256 + oc2 * 128, 128),
                                 yfull(c),
                                 start=(c == 0), stop=(c == NCH - 1)),
                             waits=([("LD", ld), self.rGem[bk].wfree()] +
                                    ([("V", yv)] if op_ == 0 and oc2 == 0 else []))
                             if c == 0 else [],
                             inc="T" if c == NCH - 1 else None)
                    tq = P.tot["T"]
                    dstT = "abuf" if oc < 8 else "bbuf"
                    cc = oc % 8
                    incs = [(self.rGem[bk].s, 1)]
                    if oc2 == 1:
                        incs.append((self.rW[sl].s, 1))
                    P.op("scalar",
                         lambda e, dstT=dstT, cc=cc, bk=bk, j=j: e.activation(
                             T[dstT].ap()[:, cc * S + 512 * j:
                                          cc * S + 512 * j + 512],
                             self.A(bk), AF.Copy, bias=0.0, scale=1.0),
                         waits=[("T", tq)], inc=incs)
                    self.rGem[bk].release()
                    if oc2 == 1:
                        self.rW[sl].release()
            ldv0 = self.load_w8_sliced("wAv", 0, 512, (0, 512), self.rW[0])
            ldv1 = self.load_w8_sliced("wAv", 4096, 512, (512, 1024), self.rW[1])
            for it in range(4):
                tt = 4 * j + it
                for half in range(2):
                    bk = 2 + half
                    ldw = ldv0 if half == 0 else ldv1
                    for c in range(NCH):
                        P.op("tensor",
                             lambda e, c=c, it=it, half=half, bk=bk: e.matmul(
                                 self.A(bk), yfull(c, it * 128, 128),
                                 self.wsl(half * 4096 + c * 512, 512),
                                 start=(c == 0), stop=(c == NCH - 1)),
                             waits=[("LD", ldw), self.rV[half].wfree()]
                             if c == 0 else [],
                             inc="T" if c == NCH - 1 else None)
                    tv = P.tot["T"]
                    incs = [(self.rV[half].s, 1)]
                    if it == 3:
                        incs.append((self.rW[half].s, 1))
                    P.op("vector",
                         lambda e, tt=tt, half=half, bk=bk: e.tensor_copy(
                             T["vbuf"].ap().rearrange("p (t h c) -> p t h c",
                                                      t=NT1, h=16, c=65)
                             [:, tt, half * 8:half * 8 + 8, 0:64],
                             self.A(bk).rearrange("p (h c) -> p h c", c=64)),
                         waits=[("T", tv)], inc=incs)
                    self.rV[half].release()
                    if it == 3:
                        self.rW[half].release()
            P.op("tensor", lambda e: e.drain(), inc=self.rYt.s)
            self.rYt.release()
            P.wait("vector", [self.rYt.wfree()])
        P.barrier()

        if DBG in ("a_prep", "noln"):
            self.dump_dbg(bi, DBG_BUF)
            return
        self.dma_g(lambda: self.wsl(0, 2048).bitcast(F32),
                   lambda: self.dram["cos128"])
        self.dma_g(lambda: self.wsl(2048, 2048).bitcast(F32),
                   lambda: self.dram["sin128"])
        self.dma_g(lambda: self.wsl(4096, 128), lambda: self.dram["rotP"])
        self.dma_g(lambda: T["eb1b"].ap()[0:16, :]
                   .rearrange("h (c n) -> h c n", n=128),
                   lambda: self.dram["eb1"].transpose([1, 0, 2]))
        self.dma_g(lambda: T["eb1b"].ap()[32:48, :]
                   .rearrange("h (c n) -> h c n", n=128),
                   lambda: self.dram["eb1"].transpose([1, 0, 2]))
        ldcs = P.tot["LD"]
        P.wait("vector", [("LD", ldcs)])
        for qk, bufn in ((0, "abuf"), (1, "bbuf")):
            buf = T[bufn]
            for j in range(NT5):
                P.wait("tensor", [self.rQS[0].wfree(), self.rQS[1].wfree()])
                for c in range(NCH):
                    P.op("tensor",
                         lambda e, c=c, buf=buf, j=j: e.matmul(
                             self.psB.ap()[0:16, 1024:1536],
                             T["e16b"].ap()[:, c * 16:(c + 1) * 16],
                             buf.ap()[:, c * S + 512 * j: c * S + 512 * j + 512],
                             start=(c == 0), stop=(c == NCH - 1)))
                for c in range(NCH):
                    vsq = P.op("vector",
                               lambda e, c=c, buf=buf, j=j: e.tensor_tensor(
                                   T["xsq"].ap(),
                                   buf.ap()[:, c * S + 512 * j:
                                            c * S + 512 * j + 512],
                                   buf.ap()[:, c * S + 512 * j:
                                            c * S + 512 * j + 512], ALU.mult),
                               waits=[self.rXq.wfree()], inc="V")
                    P.op("tensor",
                         lambda e, c=c: e.matmul(
                             self.psB.ap()[0:16, 1536:2048],
                             T["e16b"].ap()[:, c * 16:(c + 1) * 16],
                             T["xsq"].ap(),
                             start=(c == 0), stop=(c == NCH - 1)),
                         waits=[("V", vsq)], inc=self.rXq.s)
                    self.rXq.release()
                tst = P.op("tensor", lambda e: e.drain(), inc="T")
                P.op("vector", lambda e: e.tensor_copy(
                    T["hSm"].ap(), self.psB.ap()[0:16, 1024:1536]),
                    waits=[("T", tst)], inc=self.rQS[0].s)
                self.rQS[0].release()
                P.op("vector", lambda e: e.tensor_tensor(
                    T["hR"].ap(), T["hSm"].ap(), T["hSm"].ap(), ALU.mult))
                P.op("vector", lambda e: e.scalar_tensor_tensor(
                    T["hR"].ap(), self.psB.ap()[0:16, 1536:2048], 64.0,
                    T["hR"].ap(), ALU.mult, ALU.subtract), inc=self.rQS[1].s)
                self.rQS[1].release()
                vr = P.op("vector", lambda e: e.drain(), inc="V")
                P.op("scalar", lambda e: e.activation(
                    T["hR"].ap(), T["hR"].ap(), AF.Ln,
                    bias=self.T["cbias"].ap()[0:16, 5:6],
                    scale=1.0), waits=[("V", vr)])
                cr = P.op("scalar", lambda e: e.activation(
                    T["hR"].ap(), T["hR"].ap(), AF.Exp,
                    bias=self.T["cbias"].ap()[0:16, 6:7], scale=-0.5), inc="C")
                P.op("vector", lambda e: e.tensor_copy(
                    T["hRb"].ap()[0:16, :], T["hR"].ap()), waits=[("C", cr)])
                vmr = P.op("vector", lambda e: e.scalar_tensor_tensor(
                    T["hRb"].ap()[32:48, :], T["hSm"].ap(), 1.0 / 64.0,
                    T["hR"].ap(), ALU.mult, ALU.mult), inc="V")
                wa = T["pcol"].ap()[:, 8 + 2 * qk: 9 + 2 * qk]
                ba = T["pcol"].ap()[:, 9 + 2 * qk: 10 + 2 * qk]
                for c in range(NCH):
                    self.bcast_h(self.A(2), c, 0,
                                 waits=[("V", vmr), self.rQB[0].wfree()])
                    tb2 = self.bcast_h(self.A(3), c, 32,
                                       waits=[self.rQB[1].wfree()])
                    cols = slice(c * S + 512 * j, c * S + 512 * j + 512)
                    P.op("vector",
                         lambda e, buf=buf, cols=cols: e.tensor_tensor(
                             T["t1b"].ap(), buf.ap()[:, cols], self.A(2),
                             ALU.mult),
                         waits=[("T", tb2), self.rT12[0].wfree()],
                         inc=self.rQB[0].s)
                    self.rQB[0].release()
                    vz = P.op("vector",
                              lambda e: e.tensor_tensor(
                                  T["t2b"].ap(), T["t1b"].ap(), self.A(3),
                                  ALU.subtract),
                              inc=[("V", 1), (self.rQB[1].s, 1)])
                    self.rQB[1].release()
                    cz = P.op("scalar",
                              lambda e, wa=wa, ba=ba: e.activation(
                                  T["zt"].ap(), T["t2b"].ap(), AF.Identity,
                                  bias=ba, scale=wa),
                              waits=[("V", vz), self.rT12[1].wfree()],
                              inc=[("C", 1), (self.rT12[0].s, 1)])
                    self.rT12[0].release()
                    bkr = c % 2
                    trm = P.op("tensor",
                               lambda e, bkr=bkr: e.matmul(
                                   self.A(bkr), self.wsl(4096, 128),
                                   T["zt"].ap(), start=True, stop=True),
                               waits=[("C", cz), self.rGem[bkr].wfree()],
                               inc="T")
                    P.op("vector",
                         lambda e, j=j: e.tensor_tensor(
                             T["ropA"].ap(), T["zt"].ap(),
                             self.wsl(512 * j, 512), ALU.mult),
                         waits=[("C", cz)])
                    P.op("vector",
                         lambda e, j=j, bkr=bkr: e.tensor_tensor(
                             T["zt"].ap(), self.A(bkr),
                             self.wsl(2048 + 512 * j, 512), ALU.mult),
                         waits=[("T", trm)], inc=self.rGem[bkr].s)
                    self.rGem[bkr].release()
                    P.op("vector",
                         lambda e, buf=buf, cols=cols: e.tensor_tensor(
                             buf.ap()[:, cols], T["ropA"].ap(),
                             T["zt"].ap(), ALU.add),
                         inc=self.rT12[1].s)
                    self.rT12[1].release()
        P.barrier()

        if DBG == "a_ln":
            self.dump_dbg(bi, DBG_BUF)
            return
        ldp = self.load_w8("wAproj", 0, 1024)
        P.wait("tensor", [("LD", ldp)])
        for jt in range(DBG_NJT if DBG_NJT else NT1):
            q0 = 128 * jt
            prev = None
            pend = None
            for hp in range(10):
                if hp < 8:
                    kts = [k for k in (jt - 2, jt - 1, jt) if k >= 0]
                    nk = len(kts)
                    ssl = hp % 2
                    sb0 = 2 * ssl
                    scw = [self.rSc[ssl].wfree()]
                    if hp < 2:
                        scw += [self.rGem[0].wfree(), self.rGem[1].wfree()]
                    P.wait("tensor", scw)
                    for hh in range(2):
                        for i, kt in enumerate(kts):
                            col = (sb0 + hh) * 512 + i * 128
                            P.op("tensor",
                                 lambda e, hh=hh, hp=hp, kt=kt, q0=q0, col=col:
                                 e.matmul(
                                     self.psA.ap()[:, col:col + 128],
                                     T["bbuf"].ap()[64 * hh:64 * hh + 64,
                                                    hp * S + kt * 128:
                                                    hp * S + kt * 128 + 128],
                                     T["abuf"].ap()[64 * hh:64 * hh + 64,
                                                    hp * S + q0:
                                                    hp * S + q0 + 128],
                                     start=True, stop=True,
                                     tile_position=(64 * hh, 0)),
                                 inc="T" if (hh == 1 and i == nk - 1) else None)
                    tsc = P.tot["T"]
                    psl = hp % 2
                    pof = psl * 1024
                    P.op("scalar",
                         lambda e, nk=nk, pof=pof, sb0=sb0: e.activation(
                             T["probs"].ap()[:, pof: pof + nk * 128],
                             self.psA.ap()[:, sb0 * 512: sb0 * 512 + nk * 128],
                             AF.Exp, bias=self.T["cbias"].ap()[:, 0:1],
                             scale=0.125),
                         waits=[("T", tsc), self.rPr[psl].wfree()])
                    ce = P.op("scalar",
                              lambda e, nk=nk, pof=pof, sb0=sb0: e.activation(
                                  T["probs"].ap()[:, pof + 512:
                                                  pof + 512 + nk * 128],
                                  self.psA.ap()[:, (sb0 + 1) * 512:
                                                (sb0 + 1) * 512 + nk * 128],
                                  AF.Exp, bias=self.T["cbias"].ap()[:, 0:1],
                                  scale=0.125),
                              inc=[("C", 1), (self.rSc[ssl].s, 1)])
                    self.rSc[ssl].release()
                    for hh in range(2):
                        base = pof + hh * 512
                        if nk == 3:
                            P.op("vector",
                                 lambda e, base=base: e.tensor_tensor(
                                     T["probs"].ap()[:, base:base + 128],
                                     T["probs"].ap()[:, base:base + 128],
                                     T["maskU"].ap(), ALU.mult),
                                 waits=[("C", ce)] if hh == 0 else [])
                        P.op("vector",
                             lambda e, base=base, nk=nk: e.tensor_tensor(
                                 T["probs"].ap()[:, base + (nk - 1) * 128:
                                                 base + nk * 128],
                                 T["probs"].ap()[:, base + (nk - 1) * 128:
                                                 base + nk * 128],
                                 T["maskL"].ap(), ALU.mult),
                             waits=[("C", ce)] if (hh == 0 and nk != 3) else [],
                             inc="V" if hh == 1 else None)
                    cur = (hp, kts, nk, psl, pof, P.tot["V"])
                else:
                    cur = None
                if prev is not None:
                    php, pkts, pnk, ppsl, ppof, pvpm = prev
                    ob = 512 * (php % 2)
                    P.wait("tensor", [("V", pvpm), self.rOT.wfree()])
                    for hh in range(2):
                        h = 2 * php + hh
                        for i, kt in enumerate(pkts):
                            P.op("tensor",
                                 lambda e, h=h, kt=kt, i=i, pnk=pnk, ppof=ppof,
                                 ob=ob, hh=hh: e.matmul(
                                     self.psB.ap()[0:65, ob + hh * 128:
                                                   ob + hh * 128 + 128],
                                     T["vbuf"].ap()[:, (kt * 16 + h) * 65:
                                                    (kt * 16 + h) * 65 + 65],
                                     T["probs"].ap()[:, ppof + hh * 512 +
                                                     i * 128:
                                                     ppof + hh * 512 +
                                                     i * 128 + 128],
                                     start=(i == 0), stop=(i == pnk - 1)),
                                 inc=([("T", 1), (self.rPr[ppsl].s, 1)]
                                      if (hh == 1 and i == pnk - 1) else None))
                    self.rPr[ppsl].release()
                    npend = (php, ob, P.tot["T"])
                else:
                    npend = None
                if pend is not None:
                    php, ob, tpv = pend
                    rr = T["rowsC"].ap()[0:1, 0:256]
                    P.op("vector",
                         lambda e, php=php, ob=ob, rr=rr: e.tensor_tensor(
                             rr, self.psB.ap()[64:65, ob:ob + 256],
                             T["sinkrow"].ap().rearrange("o (h r) -> o h r",
                                                         r=1)
                             [:, 2 * php:2 * php + 2, :]
                             .broadcast_to([1, 2, 128]),
                             ALU.add),
                         waits=[("T", tpv), self.rRowA.wfree()])
                    vr = P.op("vector", lambda e, rr=rr: e.reciprocal(rr, rr),
                              inc="V")
                    P.op("tensor",
                         lambda e: e.matmul(
                             self.psB.ap()[0:64, 1024:1152],
                             T["onesrf"].ap()[0:1, :],
                             T["rowsC"].ap()[0:1, 0:128],
                             start=True, stop=True),
                         waits=[("V", vr), self.rBc.wfree()])
                    tb2 = P.op("tensor",
                               lambda e: e.matmul(
                                   self.psB.ap()[64:128, 1024:1152],
                                   T["onesrf"].ap()[0:1, :],
                                   T["rowsC"].ap()[0:1, 128:256],
                                   start=True, stop=True,
                                   tile_position=(0, 64)),
                               inc=[("T", 1), (self.rRowA.s, 1)])
                    self.rRowA.release()
                    P.op("scalar",
                         lambda e, ob=ob, php=php: e.activation(
                             T["outp"].ap()[0:64, php * 128:(php + 1) * 128],
                             self.psB.ap()[0:64, ob:ob + 128],
                             AF.Copy, bias=0.0, scale=1.0),
                         waits=[("T", tb2)])
                    c2 = P.op("scalar",
                              lambda e, ob=ob, php=php: e.activation(
                                  T["outp"].ap()[64:128,
                                                 php * 128:(php + 1) * 128],
                                  self.psB.ap()[0:64, ob + 128:ob + 256],
                                  AF.Copy, bias=0.0, scale=1.0),
                              inc=[("C", 1), (self.rOT.s, 1)])
                    self.rOT.release()
                    P.op("vector",
                         lambda e, php=php: e.tensor_tensor(
                             T["outp"].ap()[:, php * 128:(php + 1) * 128],
                             T["outp"].ap()[:, php * 128:(php + 1) * 128],
                             self.psB.ap()[:, 1024:1152], ALU.mult),
                         waits=[("C", c2)],
                         inc=[(self.rBc.s, 1), (self.rOp.s, 1)])
                    self.rBc.release()
                prev = cur
                pend = npend
            vop = P.tot[self.rOp.s]
            P.wait("tensor", [self.rSc[0].wfree(), self.rSc[1].wfree(),
                              (self.rOp.s, vop)])
            for dc in range(NCH):
                bk = dc % 2
                for c in range(NCH):
                    P.op("tensor",
                         lambda e, c=c, dc=dc, bk=bk: e.matmul(
                             self.psA.ap()[:, bk * 512: bk * 512 + 128],
                             self.wsl(c * 1024 + dc * 128, 128),
                             T["outp"].ap()[:, c * 128:(c + 1) * 128],
                             start=(c == 0), stop=(c == NCH - 1)),
                         waits=[self.rGem[bk].wfree()] if c == 0 else [],
                         inc="T" if c == NCH - 1 else None)
                self._res_evac(dc, jt, bi, final, P.tot["T"],
                               self.psA.ap()[:, bk * 512: bk * 512 + 128],
                               self.rGem[bk], n=128,
                               bias_col=T["pcol"].ap()[:, dc:dc + 1])
            P.op("tensor", lambda e: e.drain(), inc=self.rOp.s)
            self.rOp.release()
        P.barrier()

    def dump_dbg(self, bi, bufn="abuf"):
        P = self.P
        P.barrier()
        if bi == 0:
            src_t = self.T[bufn]
            for c in range(NCH):
                self.dma_g(lambda c=c: self.yout[0, c],
                           lambda c=c, src_t=src_t: src_t.ap()[:, c * S:(c + 1) * S],
                           inc="YD")
        P.wait("gpsimd", [("YD", P.tot.get("YD", 0))])
        P.barrier()

    def bcast_h(self, dst, c, r0, waits=()):
        return self.P.op("tensor",
                         lambda e: e.matmul(
                             dst,
                             self.T["eb1b"].ap()[r0:r0 + 16,
                                                 c * 128:(c + 1) * 128],
                             self.T["hRb"].ap()[r0:r0 + 16, :],
                             start=True, stop=True),
                         waits=waits, inc="T")

    def _res_evac(self, dc, jt, bi, final, tmm, ps, gres, n=128, bias_col=None):
        P, T = self.P, self.T
        t0 = jt * n

        def mk(dst):
            if bias_col is not None:
                return lambda e: e.scalar_tensor_tensor(
                    dst, ps, bias_col, self.sch(dc, t0, n), ALU.add, ALU.add)
            return lambda e: e.scalar_tensor_tensor(
                dst, ps, 1.0, self.sch(dc, t0, n), ALU.mult, ALU.add)
        if final:
            osl = 0
            P.op("vector", mk(T["outsb"].ap()[:, osl * 512: osl * 512 + n]),
                 waits=[("T", tmm), self.rOut[osl].wfree()],
                 inc=[(gres.s, 1), ("V", 1)])
            gres.release()
            vv = P.tot["V"]
            P.wait("sync", [("V", vv)])
            self.dma_s(lambda dc=dc, t0=t0, n=n, bi=bi: self.yout[bi, dc][:, t0:t0 + n],
                lambda osl=osl, n=n: T["outsb"].ap()[:, osl * 512: osl * 512 + n],
                inc="YD")
            P.op("sync", lambda e: e.drain(), waits=[("YD", P.tot["YD"])],
                 inc=self.rOut[osl].s)
            self.rOut[osl].release()
        else:
            P.op("vector", mk(self.sch(dc, t0, n)),
                 waits=[("T", tmm)], inc=gres.s)
            gres.release()

    def ffn(self, bi, final):
        P, T = self.P, self.T
        ldkv = self.load_w8("wFkv", 0, 1024)
        for c in range(NCH):
            self.dma_g(lambda c=c: T["expkb"].ap()[:, c * 8:(c + 1) * 8],
                       lambda c=c: self.wdram("wFs")[c])
        for c in range(NCH):
            self.dma_s(lambda c=c: T["rowsC"].ap()[:, c * 8:(c + 1) * 8],
                       lambda c=c: self.wdram("wFs")[c], inc="WL")
        self.stats_plain("Rrow", "MRrow", EPS_P)
        P.barrier()

        P.op("vector", lambda e: e.memset(
            T["vbuf"].ap().rearrange("p (a c) -> p a c", c=65)[:, :, 64:65], 1.0),
            inc="V")

        gateP = T["bbuf"].ap()
        gfeat = T["qbbuf"].ap()[0:8, 0:2048]

        lds = P.tot["LD"]
        wls = P.tot["WL"]
        P.wait("tensor", [("LD", lds)])
        P.wait("vector", [("LD", lds), ("WL", wls)])
        vwl = P.op("vector", lambda e: e.tensor_tensor(
            T["expkb"].ap()[:, 64:128], T["rowsC"].ap()[:, 0:64],
            T["expkb"].ap()[:, 0:64], ALU.subtract), inc="V")
        for g in range(2):
            for c in range(NCH):
                P.op("tensor", lambda e, g=g, c=c: e.matmul(
                    self.psB.ap()[0:1, 1024:1032], T["ones1b"].ap(),
                    T["expkb"].ap()[:, 64 * g + c * 8:64 * g + (c + 1) * 8],
                    start=(g == 0 and c == 0), stop=(g == 1 and c == NCH - 1)),
                    waits=[("V", vwl)] if (g == 0 and c == 0) else [],
                    inc="T" if (g == 1 and c == NCH - 1) else None)
        twb = P.tot["T"]
        cwb = P.op("scalar", lambda e: e.activation(
            T["expkb"].ap()[0:1, 128:136], self.psB.ap()[0:1, 1024:1032],
            AF.Copy, bias=0.0, scale=1.0 / 1024.0),
            waits=[("T", twb)], inc=[("C", 1), (self.rGt.s, 1)])
        self.rGt.release()
        PBT = [(self.psB, 256), (self.psB, 768), (self.psA, 1280),
               (self.psA, 1792)]
        for jj in range(NT5):
            ps, base = PBT[jj]
            for tl in range(4):
                for ri, rname in ((0, "Rrow"), (4, "rowsB")):
                    P.op("tensor",
                         lambda e, jj=jj, tl=tl, ri=ri, rname=rname,
                         ps=ps, base=base: e.matmul(
                             ps.ap()[:, base + ri + tl:base + ri + tl + 1],
                             T[rname].ap()[32 * jj:32 * jj + 1,
                                           tl * 128:tl * 128 + 128],
                             T["ones1b"].ap()[32 * jj:32 * jj + 1, 0:1],
                             start=True, stop=True,
                             tile_position=(32 * jj, 0)))
        P.op("tensor", lambda e: e.matmul(
            self.psB.ap()[:, 1600:1608], T["onesrb"].ap()[0:1, :],
            T["expkb"].ap()[0:1, 128:136], start=True, stop=True),
            waits=[("C", cwb)], inc="T")
        ttr = P.tot["T"]
        for jj in range(NT5):
            ps, base = PBT[jj]
            P.op("vector", lambda e, ps=ps, base=base, jj=jj: e.tensor_copy(
                T["rowsC"].ap()[:, 64 + 8 * jj:72 + 8 * jj],
                ps.ap()[:, base:base + 8]),
                waits=[("T", ttr)] if jj == 0 else [])
        P.op("vector", lambda e: e.tensor_copy(
            T["rowsC"].ap()[:, 96:104], self.psB.ap()[:, 1600:1608]))
        P.op("vector", lambda e: e.drain(), inc="V")
        vcp = P.tot["V"]
        P.wait("tensor", [("V", vcp)])
        for j in range(NT5):
            yv = self.ytile(j, "Rrow", "MRrow",
                            lambda c, j=j: self.ach(c, 512 * j, 512))
            for oc in range(4):
                bk = oc % 2
                for c in range(NCH):
                    P.op("tensor",
                         lambda e, c=c, oc=oc, bk=bk, j=j: e.matmul(
                             self.A(bk), self.wsl(c * 1024 + oc * 128, 128),
                             self.ach(c, 512 * j, 512),
                             start=(c == 0), stop=(c == NCH - 1)),
                         waits=([self.rGem[bk].wfree()] +
                                ([("V", yv)] if oc == 0 else []))
                         if c == 0 else [],
                         inc="T" if c == NCH - 1 else None)
                tk = P.tot["T"]
                P.op("scalar",
                     lambda e, oc=oc, bk=bk, j=j: e.activation(
                         self.kch(oc, 512 * j, 512), self.A(bk), AF.Copy,
                         bias=0.0, scale=1.0),
                     waits=[("T", tk)], inc=self.rGem[bk].s)
                self.rGem[bk].release()
            for it in range(4):
                tt = 4 * j + it
                bk = 2 + it % 2
                for c in range(NCH):
                    P.op("tensor",
                         lambda e, c=c, it=it, j=j, bk=bk: e.matmul(
                             self.A(bk),
                             self.ach(c, 512 * j + it * 128, 128),
                             self.wsl(c * 1024 + 512, 512),
                             start=(c == 0), stop=(c == NCH - 1)),
                         waits=[self.rV[it % 2].wfree(), self.rGt.wfree()]
                         if c == 0 else [])
                    P.op("tensor",
                         lambda e, c=c, it=it, j=j: e.matmul(
                             self.psB.ap()[0:128, 1024 + 8 * it:
                                           1032 + 8 * it],
                             self.sch(c, 512 * j + it * 128, 128),
                             T["expkb"].ap()[:, c * 8:(c + 1) * 8],
                             start=(c == 0), stop=False))
                for c in range(NCH):
                    P.op("tensor",
                         lambda e, c=c, it=it, j=j: e.matmul(
                             self.psB.ap()[0:128, 1024 + 8 * it:
                                           1032 + 8 * it],
                             self.sch(c, 512 * j + it * 128, 128),
                             T["expkb"].ap()[:, 64 + c * 8:64 + (c + 1) * 8],
                             start=False, stop=(c == NCH - 1)),
                         inc="T" if c == NCH - 1 else None)
                tv = P.tot["T"]
                P.op("vector",
                     lambda e, tt=tt, bk=bk: e.tensor_copy(
                         T["vbuf"].ap().rearrange("p (t h c) -> p t h c",
                                                  t=NT1, h=16, c=65)
                         [:, tt, 0:8, 0:64],
                         self.A(bk).rearrange("p (h c) -> p h c", c=64)),
                     waits=[("T", tv)], inc=self.rV[it % 2].s)
                self.rV[it % 2].release()
                rcS = T["rowsC"].ap()[:, 68 + 8 * j + it:69 + 8 * j + it]
                P.op("vector",
                     lambda e, rcS=rcS, it=it: e.scalar_tensor_tensor(
                         T["rowsC"].ap()[:, 136 + 8 * it:144 + 8 * it],
                         T["rowsC"].ap()[:, 96:104], rcS,
                         self.psB.ap()[0:128, 1024 + 8 * it:1032 + 8 * it],
                         ALU.mult, ALU.subtract),
                     waits=[("T", tv)], inc=self.rGt.s)
                self.rGt.release()
            gs32 = T["rowsC"].ap()[:, 136:168]
            P.op("vector", lambda e: e.drain())
            for it2 in range(4):
                rcR = T["rowsC"].ap()[:, 64 + 8 * j + it2:65 + 8 * j + it2]
                P.op("vector",
                     lambda e, rcR=rcR, it2=it2: e.tensor_scalar(
                         T["rowsC"].ap()[:, 136 + 8 * it2:144 + 8 * it2],
                         T["rowsC"].ap()[:, 136 + 8 * it2:144 + 8 * it2],
                         rcR, None, ALU.mult),
                     inc="V" if it2 == 3 else None)
            vsc = P.tot["V"]
            cg = P.op("scalar",
                      lambda e, gs32=gs32: e.activation(
                          gs32, gs32, AF.Exp, bias=0.0, scale=1.0),
                      waits=[("V", vsc)], inc="C")
            P.op("vector", lambda e, gs32=gs32: e.tensor_scalar(
                gs32, gs32, 1.0, None, ALU.add), waits=[("C", cg)])
            P.op("vector", lambda e: e.drain())
            P.op("vector", lambda e, gs32=gs32: e.reciprocal(gs32, gs32))
            P.op("vector", lambda e: e.drain())
            for it2 in range(4):
                P.op("vector", lambda e, it2=it2: e.max(
                    T["rowsC"].ap()[:, 104 + 8 * it2:112 + 8 * it2],
                    T["rowsC"].ap()[:, 136 + 8 * it2:144 + 8 * it2]))
            P.op("vector", lambda e: e.drain())
            for it2 in range(4):
                P.op("vector", lambda e, it2=it2: e.tensor_scalar(
                    T["rowsC"].ap()[:, 104 + 8 * it2:112 + 8 * it2],
                    T["rowsC"].ap()[:, 136 + 8 * it2:144 + 8 * it2],
                    T["rowsC"].ap()[:, 106 + 8 * it2:107 + 8 * it2], None,
                    ALU.is_ge))
            P.op("vector", lambda e: e.drain())
            for it2 in range(4):
                P.op("vector", lambda e, it2=it2: e.tensor_tensor(
                    T["gtmp"].ap()[:, it2 * 8:(it2 + 1) * 8],
                    T["rowsC"].ap()[:, 136 + 8 * it2:144 + 8 * it2],
                    T["rowsC"].ap()[:, 104 + 8 * it2:112 + 8 * it2],
                    ALU.mult), inc="V" if it2 == 3 else None)
            vg = P.tot["V"]
            for h in range(8):
                tg = P.op("tensor",
                          lambda e, h=h: e.transpose(
                              self.psB.ap()[0:4, 1536:1600].bitcast(BF16),
                              T["gtmp"].ap()
                              .rearrange("p (a b) -> p a b", b=8)[:, :, h],
                              T["identb"].ap()),
                          waits=[("V", vg), self.rBc.wfree()], inc="T")
                vv = P.op("vector",
                          lambda e: e.tensor_copy(
                              T["gtr"].ap()[:, 0:128],
                              self.psB.ap()[0:4, 1536:1600].bitcast(BF16)),
                          waits=[("T", tg), ("WL", P.tot.get("WL", 0))],
                          inc=[("V", 1), (self.rBc.s, 1)])
                self.rBc.release()
                P.wait("sync", [("V", vv)])
                self.dma_s(lambda h=h, j=j: gateP[32 * (h % 2):32 * (h % 2) + 1,
                                                  (h // 2) * 2048 + 512 * j:
                                                  (h // 2) * 2048 + 512 * j + 512]
                           .rearrange("o (a b) -> o a b", a=4),
                           lambda: T["gtr"].ap()[:, 0:128], inc="WL")
                self.dma_s(lambda h=h, j=j: gfeat[h:h + 1,
                                                  512 * j:512 * j + 512]
                           .rearrange("o (a b) -> o a b", a=4),
                           lambda: T["gtr"].ap()[:, 0:128], inc="WL")
                P.wait("vector", [("WL", P.tot["WL"])])
        P.barrier()

        if DBG == "f_prep":
            P.barrier()
            if bi == 0:
                for c in range(4):
                    self.dma_g(lambda c=c: self.yout[0, c],
                               lambda c=c: self.kch(c, 0, S), inc="YD")
                self.dma_g(lambda: self.yout[0, 4],
                           lambda: T["qbbuf"].ap()[:, 0:2048], inc="YD")
                self.dma_g(lambda: self.yout[0, 5],
                           lambda: T["bbuf"].ap()[:, 0:2048], inc="YD")
                self.dma_g(lambda: self.yout[0, 6],
                           lambda: T["bbuf"].ap()[:, 2048:4096], inc="YD")
            P.wait("gpsimd", [("YD", P.tot.get("YD", 0))])
            P.barrier()
            return
        ldq = self.load_w8("wFq", 0, 512)
        ldo = self.load_w8("wFo", 4096, 1024, nch=4)
        self.dma_g(lambda: T["boFb"].ap(), lambda: self.dram["boF"])
        P.wait("tensor", [("LD", P.tot["LD"]), ("WL", P.tot["WL"])])
        for jq in range(NT5):
            q0 = 512 * jq
            for oc in range(4):
                bk = oc % 2
                for c in range(NCH):
                    P.op("tensor",
                         lambda e, c=c, oc=oc, bk=bk, q0=q0: e.matmul(
                             self.A(bk), self.wsl(c * 512 + oc * 128, 128),
                             self.ach(c, q0, 512),
                             start=(c == 0), stop=(c == NCH - 1)),
                         waits=[self.rGem[bk].wfree(), self.rQt.wfree(),
                                self.rSc[bk].wfree()]
                         if c == 0 else [],
                         inc="T" if c == NCH - 1 else None)
                tq = P.tot["T"]
                P.op("scalar",
                     lambda e, oc=oc, bk=bk: e.activation(
                         T["qbuf"].ap()[:, oc * 512:(oc + 1) * 512],
                         self.A(bk), AF.Copy, bias=0.0, scale=1.0),
                     waits=[("T", tq)], inc=self.rGem[bk].s)
                self.rGem[bk].release()
            cq = P.op("scalar", lambda e: e.drain(), inc="C")
            for hp in range(4):
                self._attn_core(hp=hp, jq=jq, kbuf=self.kch, qwait=cq,
                                scale=0.125, shiftcol=1, aff=False,
                                gate=(gateP, 2048), nkt=NT1)
            self._flush_epi()
            vop = P.tot[self.rOp.s]
            P.wait("tensor", [self.rSc[0].wfree(), self.rSc[1].wfree(),
                              (self.rOp.s, vop)])
            for dc in range(NCH):
                bk = dc % 2
                for hc in range(4):
                    P.op("tensor",
                         lambda e, hc=hc, dc=dc, bk=bk: e.matmul(
                             self.A(bk),
                             self.wsl(4096 + hc * 1024 + dc * 128, 128),
                             T["outp"].ap()[:, hc * 512:(hc + 1) * 512],
                             start=(hc == 0), stop=False),
                         waits=[self.rGem[bk].wfree()] if hc == 0 else [])
                P.op("tensor",
                     lambda e, dc=dc, bk=bk, q0=q0: e.matmul(
                         self.A(bk), T["boFb"].ap()[:, dc * 128:(dc + 1) * 128],
                         gfeat[:, q0:q0 + 512],
                         start=False, stop=True),
                     inc="T")
                self._res_evac(dc, jq, bi, final, P.tot["T"],
                               self.A(bk), self.rGem[bk], n=512)
            P.op("tensor", lambda e: e.drain(), inc=self.rOp.s)
            self.rOp.release()
        P.barrier()

    def _pv(self, hp, kt, ce, nkt):
        P, T = self.P, self.T
        P.wait("tensor", [("C", ce)])
        psl = kt % 2
        pof = psl * 1024
        for hh in range(2):
            h = 2 * hp + hh
            P.op("tensor",
                 lambda e, h=h, kt=kt, hh=hh, pof=pof: e.matmul(
                     self.psB.ap()[0:65, hh * 512:hh * 512 + 512],
                     T["vbuf"].ap()[:, (kt * 16 + h) * 65:
                                    (kt * 16 + h) * 65 + 65],
                     T["probs"].ap()[:, pof + hh * 512:pof + hh * 512 + 512],
                     start=(kt == 0), stop=(kt == nkt - 1)),
                 inc=([("T", 1), (self.rPr[psl].s, 1)] if hh == 1 else None))
        self.rPr[psl].release()

    def _attn_core(self, hp, jq, kbuf, qwait, scale, shiftcol, aff, gate, nkt):
        P, T = self.P, self.T
        q0 = 512 * jq
        ce_q = []
        for kt in range(nkt):
            sb0 = 2 * (kt % 2)
            if kt == 0:
                scw = [self.rSc[0].wfree(), self.rSc[1].wfree()]
                if hp == 0:
                    scw += [self.rGem[0].wfree(), self.rGem[1].wfree()]
                P.wait("tensor", scw)
            for hh in range(2):
                P.op("tensor",
                     lambda e, hh=hh, hp=hp, kt=kt, sb0=sb0: e.matmul(
                         self.A(sb0 + hh),
                         kbuf(hp, kt * 128, 128)[64 * hh:64 * hh + 64, :],
                         T["qbuf"].ap()[64 * hh:64 * hh + 64,
                                        hp * 512:(hp + 1) * 512],
                         start=True, stop=(not aff),
                         tile_position=(64 * hh, 0)),
                     waits=[("C", qwait)] if (kt == 0 and hh == 0) else [])
            if aff:
                for hh in range(2):
                    h = 2 * hp + hh
                    rb = 64 * (h % 2) + 32 * ((h % 4) // 2)
                    P.op("tensor",
                         lambda e, h=h, rb=rb, kt=kt, hh=hh, sb0=sb0: e.matmul(
                             self.A(sb0 + hh),
                             T["kbbuf"].ap()[rb:rb + 32,
                                             (h // 4) * S + kt * 128:
                                             (h // 4) * S + kt * 128 + 128],
                             T["qbbuf"].ap()[rb:rb + 32,
                                             (h // 4) * 512:
                                             (h // 4) * 512 + 512],
                             start=False, stop=True,
                             tile_position=(rb, 0)),
                         inc="T" if hh == 1 else None)
            else:
                P.op("tensor", lambda e: e.drain(), inc="T")
            tsc = P.tot["T"]
            psl = kt % 2
            pof = psl * 1024
            ce = P.op("scalar",
                      lambda e, pof=pof, sb0=sb0, scale=scale,
                      shiftcol=shiftcol: e.activation(
                          T["probs"].ap()[:, pof:pof + 1024],
                          self.psA.ap()[:, sb0 * 512: sb0 * 512 + 1024],
                          AF.Exp, bias=T["cbias"].ap()[:, shiftcol:shiftcol + 1],
                          scale=scale),
                      waits=[("T", tsc), self.rPr[psl].wfree()],
                      inc=[("C", 1), (self.rSc[kt % 2].s, 1)])
            self.rSc[kt % 2].release()
            ce_q.append(ce)
            if kt == 1:
                self._flush_epi()
                P.wait("tensor", [self.rOT.wfree()])
            if kt > 0:
                self._pv(hp, kt - 1, ce_q[kt - 1], nkt)
        self._pv(hp, nkt - 1, ce_q[nkt - 1], nkt)
        self._epi_pend = (hp, jq, gate, P.tot["T"])

    def _flush_epi(self):
        if self._epi_pend is None:
            return
        hp, jq, gate, tpv = self._epi_pend
        self._epi_pend = None
        self._attn_epilogue(hp, jq, gate, tpv)

    def _attn_epilogue(self, hp, jq, gate, tpv):
        P, T = self.P, self.T
        q0 = 512 * jq
        s1 = T["rowsC"].ap()[0:1, :]
        s2 = T["rowsC"].ap()[32:33, :]
        P.wait("vector", [("T", tpv), self.rRowA.wfree()])
        P.op("vector", lambda e, s1=s1: e.reciprocal(
            s1, self.psB.ap()[64:65, 0:512]))
        vr = P.op("vector", lambda e, s2=s2: e.reciprocal(
            s2, self.psB.ap()[64:65, 512:1024]), inc="V")
        if gate is not None:
            gp, gstride = gate
            for hh, sr in ((0, s1), (1, s2)):
                h = 2 * hp + hh
                P.op("vector",
                     lambda e, h=h, sr=sr, gp=gp, gstride=gstride, q0=q0:
                     e.tensor_tensor(
                         sr, sr, gp[32 * (h % 2):32 * (h % 2) + 1,
                                    (h // 2) * gstride + q0:
                                    (h // 2) * gstride + q0 + 512], ALU.mult),
                     inc="V" if hh == 1 else None)
            vr = P.tot["V"]
        P.op("tensor",
             lambda e, s1=s1: e.matmul(
                 self.psB.ap()[0:64, 1024:1536],
                 T["onesrf"].ap()[0:1, :],
                 s1, start=True, stop=True),
             waits=[("V", vr), self.rBc.wfree()])
        tb2 = P.op("tensor",
                   lambda e, s2=s2: e.matmul(
                       self.psB.ap()[64:128, 1024:1536],
                       T["onesrf"].ap()[32:33, :],
                       s2, start=True, stop=True, tile_position=(32, 64)),
                   inc=[("T", 1), (self.rRowA.s, 1)])
        self.rRowA.release()
        hc = hp
        P.op("scalar",
             lambda e, hc=hc: e.activation(
                 T["outp"].ap()[0:64, hc * 512:(hc + 1) * 512],
                 self.psB.ap()[0:64, 0:512], AF.Copy, bias=0.0, scale=1.0),
             waits=[("T", tb2)])
        c2 = P.op("scalar",
                  lambda e, hc=hc: e.activation(
                      T["outp"].ap()[64:128, hc * 512:(hc + 1) * 512],
                      self.psB.ap()[0:64, 512:1024], AF.Copy, bias=0.0,
                      scale=1.0),
                  inc=[("C", 1), (self.rOT.s, 1)])
        self.rOT.release()
        P.op("vector",
             lambda e, hc=hc: e.tensor_tensor(
                 T["outp"].ap()[:, hc * 512:(hc + 1) * 512],
                 T["outp"].ap()[:, hc * 512:(hc + 1) * 512],
                 self.psB.ap()[:, 1024:1536], ALU.mult),
             waits=[("C", c2)],
             inc=[(self.rBc.s, 1), (self.rOp.s, 1)])
        self.rBc.release()

    def attn_b(self, bi, final):
        P, T = self.P, self.T
        ldk = self.load_w8("wBk", 0, 1024)
        vst = self.stats_plain("Rrow", "MRrow", EPS_P, hold_rows=True)
        P.op("vector", lambda e: e.tensor_tensor(
            T["rowsA"].ap(), T["rowsA"].ap(), T["rowsA"].ap(), ALU.mult),
            waits=[("V", vst)])
        P.op("vector", lambda e: e.tensor_scalar(
            T["rowsA"].ap(), T["rowsA"].ap(), -EPS_P, 1.0 + EPS_2,
            ALU.mult, ALU.add))
        P.op("vector", lambda e: e.drain())
        P.op("vector", lambda e: e.tensor_scalar(
            T["rowsA"].ap(), T["rowsA"].ap(), EPS_2, None, ALU.max), inc="V")
        vv = P.tot["V"]
        P.op("scalar", lambda e: e.activation(
            T["rowsA"].ap(), T["rowsA"].ap(), AF.Ln, bias=0.0, scale=1.0),
            waits=[("V", vv)])
        cr = P.op("scalar", lambda e: e.activation(
            T["rowsA"].ap(), T["rowsA"].ap(), AF.Exp, bias=0.0, scale=-0.5),
            inc="C")
        P.op("vector", lambda e: e.tensor_tensor(
            T["Arow"].ap(), T["Rrow"].ap(), T["rowsA"].ap(), ALU.mult),
            waits=[("C", cr)])
        P.op("vector", lambda e: e.tensor_tensor(
            T["MArow"].ap(), T["MRrow"].ap(), T["rowsA"].ap(), ALU.mult),
            inc=[("V", 1), (self.rRowA.s, 1)])
        self.rRowA.release()
        P.barrier()

        P.op("vector", lambda e: e.memset(
            T["vbuf"].ap().rearrange("p (a c) -> p a c", c=65)[:, :, 64:65], 1.0),
            inc="V")

        P.wait("tensor", [("LD", ldk)])
        for j in range(NT5):
            cv = self.y1tile(j, lambda c: T["ybuf"].ap()[:, c * 512:(c + 1) * 512])
            for oc in range(NCH):
                bk = oc % 2
                for c in range(NCH):
                    P.op("tensor",
                         lambda e, c=c, oc=oc, bk=bk: e.matmul(
                             self.A(bk), self.wsl(c * 1024 + oc * 128, 128),
                             T["ybuf"].ap()[:, c * 512:(c + 1) * 512],
                             start=(c == 0), stop=(c == NCH - 1)),
                         waits=([self.rGem[bk].wfree()] +
                                ([("C", cv)] if oc == 0 else []))
                         if c == 0 else [],
                         inc="T" if c == NCH - 1 else None)
                tk = P.tot["T"]
                incs = [(self.rGem[bk].s, 1)]
                if oc == NCH - 1:
                    incs.append((self.rYt.s, 1))
                P.op("scalar",
                     lambda e, oc=oc, bk=bk, j=j: e.activation(
                         self.ach(oc, 512 * j, 512), self.A(bk), AF.Copy,
                         bias=0.0, scale=1.0),
                     waits=[("T", tk)], inc=incs)
                self.rGem[bk].release()
            self.rYt.release()
            P.wait("vector", [self.rYt.wfree()])
            P.wait("scalar", [self.rYt.wfree()])
        P.barrier()
        ldv = self.load_w8("wBv", 0, 1024)
        P.wait("tensor", [("LD", ldv)])
        for j in range(NT5):
            cv = self.y1tile(j, lambda c: T["ybuf"].ap()[:, c * 512:(c + 1) * 512])
            for it in range(4):
                tt = 4 * j + it
                for half in range(2):
                    bk = 2 + half
                    for c in range(NCH):
                        P.op("tensor",
                             lambda e, c=c, it=it, half=half, bk=bk: e.matmul(
                                 self.A(bk),
                                 T["ybuf"].ap()[:, c * 512 + it * 128:
                                                c * 512 + it * 128 + 128],
                                 self.wsl(c * 1024 + half * 512, 512),
                                 start=(c == 0), stop=(c == NCH - 1)),
                             waits=([self.rV[half].wfree()] +
                                    ([("C", cv)] if it == 0 else []))
                             if c == 0 else [],
                             inc="T" if c == NCH - 1 else None)
                    tv = P.tot["T"]
                    incs = [(self.rV[half].s, 1)]
                    if it == 3:
                        incs.append((self.rYt.s, 1))
                    P.op("vector",
                         lambda e, tt=tt, half=half, bk=bk: e.tensor_copy(
                             T["vbuf"].ap().rearrange("p (t h c) -> p t h c",
                                                      t=NT1, h=16, c=65)
                             [:, tt, half * 8:half * 8 + 8, 0:64],
                             self.A(bk).rearrange("p (h c) -> p h c", c=64)),
                         waits=[("T", tv)], inc=incs)
                    self.rV[half].release()
            self.rYt.release()
            P.wait("vector", [self.rYt.wfree()])
            P.wait("scalar", [self.rYt.wfree()])
        P.barrier()
        self.dma_g(lambda: self.wsl(0, 256).bitcast(F32),
                   lambda: self.dram["wBhk"])
        self.dma_g(lambda: self.wsl(256, 256).bitcast(F32),
                   lambda: self.dram["wBhq"])
        ldh = P.tot["LD"]
        P.wait("tensor", [("LD", ldh)])
        for j in range(NT5):
            for g4 in range(4):
                self._cluster_softmax(
                    src=lambda hh, g4=g4, j=j: self.ach(
                        g4 * 2 + hh // 2, 512 * j, 512)
                    [64 * (hh % 2):64 * (hh % 2) + 64, :],
                    woff=0, g4=g4, j=j, is_q=False)
        P.barrier()

        self.dma_g(lambda: self.wsl(4096, 256).bitcast(F32),
                   lambda: self.dram["wBhq"])
        for jq in range(NT5):
            q0 = 512 * jq
            cv = self.y1tile(jq, lambda c: T["ybuf"].ap()[:, c * 512:(c + 1) * 512])
            for op_ in range(4):
                sl = op_ % 2
                ld = self.load_w8_sliced("wBq", sl * 2048, 256,
                                         (op_ * 256, op_ * 256 + 256),
                                         self.rW[sl])
                for oc2 in range(2):
                    oc = op_ * 2 + oc2
                    bk = oc % 2
                    for c in range(NCH):
                        P.op("tensor",
                             lambda e, c=c, oc2=oc2, sl=sl, bk=bk: e.matmul(
                                 self.A(bk),
                                 self.wsl(sl * 2048 + c * 256 + oc2 * 128, 128),
                                 T["ybuf"].ap()[:, c * 512:(c + 1) * 512],
                                 start=(c == 0), stop=(c == NCH - 1)),
                             waits=([("LD", ld), self.rGem[bk].wfree(),
                                     self.rQt.wfree(), self.rSc[bk].wfree()] +
                                    ([("C", cv)] if op_ == 0 and oc2 == 0
                                     else []))
                             if c == 0 else [],
                             inc="T" if c == NCH - 1 else None)
                    tq = P.tot["T"]
                    incs = [(self.rGem[bk].s, 1)]
                    if oc2 == 1:
                        incs.append((self.rW[sl].s, 1))
                    if oc == 7:
                        incs.append((self.rYt.s, 1))
                    P.op("scalar",
                         lambda e, oc=oc, bk=bk: e.activation(
                             T["qbuf"].ap()[:, oc * 512:(oc + 1) * 512],
                             self.A(bk), AF.Copy, bias=0.0, scale=0.125),
                         waits=[("T", tq)], inc=incs)
                    self.rGem[bk].release()
                    if oc2 == 1:
                        self.rW[sl].release()
            self.rYt.release()
            cq = P.op("scalar", lambda e: e.drain(), inc="C")
            for g4 in range(4):
                self._cluster_softmax(
                    src=lambda hh, g4=g4: T["qbuf"].ap()
                    [64 * (hh % 2):64 * (hh % 2) + 64,
                     (g4 * 2 + hh // 2) * 512:(g4 * 2 + hh // 2) * 512 + 512],
                    woff=4096, g4=g4, j=jq, is_q=True)
            for hp in range(8):
                self._attn_core(hp=hp, jq=jq, kbuf=self.ach, qwait=cq,
                                scale=1.0, shiftcol=2, aff=True, gate=None,
                                nkt=NT1)
            self._flush_epi()
            vop = P.tot[self.rOp.s]
            P.wait("tensor", [self.rSc[0].wfree(), self.rSc[1].wfree(),
                              (self.rOp.s, vop)])
            ldos = [self.load_w8_sliced(
                "wBo", 4352 + (d0 % 2) * 1024, 128,
                (d0 * 128, d0 * 128 + 128), self.rW[d0 % 2])
                for d0 in range(2)]
            for dc in range(NCH):
                sl = dc % 2
                ldo = ldos[dc]
                bk = dc % 2
                for c in range(NCH):
                    P.op("tensor",
                         lambda e, c=c, bk=bk, sl=sl: e.matmul(
                             self.A(bk), self.wsl(4352 + sl * 1024 + c * 128, 128),
                             T["outp"].ap()[:, c * 512:(c + 1) * 512],
                             start=(c == 0), stop=(c == NCH - 1)),
                         waits=[("LD", ldo), self.rGem[bk].wfree()]
                         if c == 0 else [],
                         inc="T" if c == NCH - 1 else None)
                two = P.tot["T"]
                tb = "t1b" if sl == 0 else "t2b"
                self.bcast(self.Bk(2), "Rrow", jq, 0, 512,
                           waits=[self.rYb[0].wfree()])
                tM = self.bcast(self.psB.ap()[:, 1536:2048], "MRrow", jq, 0, 512,
                                waits=[self.rYb[1].wfree()])
                P.op("vector",
                     lambda e, dc=dc, jq=jq, tb=tb: e.tensor_tensor(
                         T[tb].ap(), self.sch(dc, 512 * jq, 512),
                         self.Bk(2), ALU.mult),
                     waits=[("T", tM), self.rT12[sl].wfree()],
                     inc=self.rYb[0].s)
                self.rYb[0].release()
                P.op("vector",
                     lambda e, tb=tb: e.tensor_tensor(
                         T[tb].ap(), T[tb].ap(),
                         self.psB.ap()[:, 1536:2048], ALU.subtract),
                     inc=self.rYb[1].s)
                self.rYb[1].release()
                P.op("vector",
                     lambda e, dc=dc, jq=jq, bk=bk, tb=tb:
                     e.scalar_tensor_tensor(
                         self.bch(dc, 512 * jq, 512), self.A(bk), 1.0,
                         T[tb].ap(), ALU.mult, ALU.add),
                     inc=[(self.rGem[bk].s, 1), (self.rT12[sl].s, 1),
                          (self.rW[sl].s, 1)],
                     waits=[("T", two)])
                self.rGem[bk].release()
                self.rT12[sl].release()
                self.rW[sl].release()
                if dc + 2 < NCH:
                    ldos.append(self.load_w8_sliced(
                        "wBo", 4352 + sl * 1024, 128,
                        ((dc + 2) * 128, (dc + 2) * 128 + 128), self.rW[sl]))
            P.op("tensor", lambda e: e.drain(), inc=self.rOp.s)
            self.rOp.release()
        P.barrier()

        self.stats_plain("R2row", "MR2row", EPS_2, srcfn=self.bch)
        P.barrier()
        for j in range(NT5):
            self.bcast(self.Bk(0), "R2row", j, 0, 512, waits=[self.rYb[0].wfree()])
            tM = self.bcast(self.Bk(1), "MR2row", j, 0, 512,
                            waits=[self.rYb[1].wfree()])
            for c in range(NCH):
                sl = c % 2
                tb = "t1b" if sl == 0 else "t2b"
                P.op("vector",
                     lambda e, c=c, j=j, tb=tb: e.tensor_tensor(
                         T[tb].ap(), self.bch(c, 512 * j, 512), self.Bk(0),
                         ALU.mult),
                     waits=([("T", tM)] if c == 0 else []) +
                     [self.rT12[sl].wfree()])
                vv = P.op("vector",
                          lambda e, tb=tb: e.tensor_tensor(
                              T[tb].ap(), T[tb].ap(), self.Bk(1), ALU.subtract),
                          inc=([("V", 1), (self.rYb[0].s, 1),
                                (self.rYb[1].s, 1)] if c == NCH - 1 else "V"))
                P.op("scalar",
                     lambda e, c=c, tb=tb: e.activation(
                         T["ybuf"].ap()[:, c * 512:(c + 1) * 512],
                         T[tb].ap(), AF.Identity,
                         bias=T["gcol"].ap()[:, 24 + c:25 + c],
                         scale=T["gcol"].ap()[:, 16 + c:17 + c]),
                     waits=[("V", vv)], inc=[("C", 1), (self.rT12[sl].s, 1)])
                self.rT12[sl].release()
            self.rYb[0].release()
            self.rYb[1].release()
            cy = P.tot["C"]
            for m in range(32):
                sl = m % 4
                ldm = self.load_w1("wB1", sl * 1024, m, self.rW[sl])
                bk = m % 2
                for c in range(NCH):
                    P.op("tensor",
                         lambda e, c=c, sl=sl, bk=bk: e.matmul(
                             self.A(bk), self.wsl(sl * 1024 + c * 128, 128),
                             T["ybuf"].ap()[:, c * 512:(c + 1) * 512],
                             start=(c == 0), stop=(c == NCH - 1)),
                         waits=([("LD", ldm), self.rGem[bk].wfree()] +
                                ([("C", cy)] if m == 0 else []))
                         if c == 0 else [],
                         inc="T" if c == NCH - 1 else None)
                th = P.tot["T"]
                P.op("scalar",
                     lambda e, m=m, bk=bk: e.activation(
                         T["abuf"].ap()[:, m * 512:(m + 1) * 512],
                         self.A(bk), AF.Relu,
                         bias=T["b1col"].ap()[:, m:m + 1], scale=1.0),
                     waits=[("T", th)],
                     inc=[(self.rGem[bk].s, 1), (self.rW[sl].s, 1)])
                self.rGem[bk].release()
                self.rW[sl].release()
            ch1 = P.op("scalar", lambda e: e.drain(), inc="C")
            P.wait("gpsimd", [self.rW[0].wfree(), self.rW[1].wfree(),
                              self.rW[2].wfree(), self.rW[3].wfree()])
            ldms = [self.load_w1("wB2", (d0 % 2) * 4096, d0,
                                 self.rW[d0 % 2]) for d0 in range(2)]
            for dc in range(NCH):
                sl = dc % 2
                ldm = ldms[dc]
                bk = 2 + dc % 2
                for m in range(32):
                    P.op("tensor",
                         lambda e, m=m, sl=sl, bk=bk: e.matmul(
                             self.A(bk), self.wsl(sl * 4096 + m * 128, 128),
                             T["abuf"].ap()[:, m * 512:(m + 1) * 512],
                             start=(m == 0), stop=(m == 31)),
                         waits=([("LD", ldm), self.rV[dc % 2].wfree()] +
                                ([("C", ch1)] if dc == 0 else []))
                         if m == 0 else [],
                         inc="T" if m == 31 else None)
                tm2 = P.tot["T"]
                sl2 = dc % 2
                tb = "t1b" if sl2 == 0 else "t2b"
                P.op("vector",
                     lambda e, dc=dc, j=j, bk=bk, tb=tb: e.scalar_tensor_tensor(
                         T[tb].ap(), self.A(bk),
                         T["pcol"].ap()[:, 12 + dc:13 + dc],
                         self.bch(dc, 512 * j, 512), ALU.add, ALU.add),
                     waits=[("T", tm2), self.rT12[sl2].wfree()],
                     inc=self.rV[dc % 2].s)
                self.rV[dc % 2].release()
                P.op("vector",
                     lambda e, dc=dc, j=j, tb=tb: e.tensor_tensor(
                         self.sch(dc, 512 * j, 512), T[tb].ap(),
                         self.sch(dc, 512 * j, 512), ALU.add),
                     inc=[(self.rT12[sl2].s, 1), (self.rW[sl].s, 1)])
                self.rT12[sl2].release()
                self.rW[sl].release()
                if dc + 2 < NCH:
                    ldms.append(self.load_w1("wB2", sl * 4096, dc + 2,
                                             self.rW[sl]))
        P.barrier()

    def _cluster_softmax(self, src, woff, g4, j, is_q):
        RB = (0, 64, 32, 96)
        P, T = self.P, self.T
        P.wait("tensor", [self.rKb.wfree(), self.rBc.wfree()])
        for hh in range(4):
            hq = 4 * g4 + hh
            pair = hq // 2
            bko = 1536 if hh % 2 == 0 else 1024
            P.op("tensor",
                 lambda e, hh=hh, hq=hq, pair=pair, woff=woff, bko=bko:
                 e.matmul(
                     self.psB.ap()[RB[hh]:RB[hh] + 32, bko:bko + 512],
                     self.wsl(woff + pair * 32, 32)
                     [64 * (hq % 2):64 * (hq % 2) + 64, :],
                     src(hh), start=True, stop=True,
                     tile_position=(64 * (hq % 2), RB[hh])),
                 inc="T" if hh == 3 else None)
        tlg = P.tot["T"]
        P.op("scalar",
             lambda e: e.activation(T["expkb"].ap()[0:64, :],
                                    self.psB.ap()[0:64, 1536:2048],
                                    AF.Exp, bias=0.0, scale=1.0),
             waits=[("T", tlg)])
        ce = P.op("scalar",
                  lambda e: e.activation(T["expkb"].ap()[64:128, :],
                                         self.psB.ap()[64:128, 1024:1536],
                                         AF.Exp, bias=0.0, scale=1.0),
                  inc="C")
        td = P.op("tensor",
                  lambda e: e.matmul(self.psB.ap()[0:4, 1536:2048],
                                     T["e4b16"].ap(), T["expkb"].ap(),
                                     start=True, stop=True),
                  waits=[("C", ce)], inc="T")
        vr = P.op("vector",
                  lambda e: e.reciprocal(T["rowsC"].ap()[64:68, :],
                                         self.psB.ap()[0:4, 1536:2048]),
                  waits=[("T", td), self.rRowA.wfree()], inc="V")
        vb = P.op("vector",
                  lambda e: e.tensor_copy(T["hRb"].ap()[0:4, :],
                                          T["rowsC"].ap()[64:68, :]),
                  waits=[("V", vr)], inc=[("V", 1), (self.rRowA.s, 1)])
        self.rRowA.release()
        tb = P.op("tensor",
                  lambda e: e.matmul(self.psB.ap()[:, 1536:2048],
                                     T["e4bb"].ap(), T["hRb"].ap()[0:4, :],
                                     start=True, stop=True),
                  waits=[("V", vb)], inc="T")
        if is_q:
            P.op("vector",
                 lambda e, g4=g4: e.tensor_tensor(
                     T["qbbuf"].ap()[:, g4 * 512:(g4 + 1) * 512],
                     T["expkb"].ap(), self.psB.ap()[:, 1536:2048], ALU.mult),
                 waits=[("T", tb)],
                 inc=[(self.rKb.s, 1), (self.rBc.s, 1)])
        else:
            P.op("vector",
                 lambda e: e.tensor_tensor(
                     T["expkb"].ap(), T["expkb"].ap(),
                     self.psB.ap()[:, 1536:2048], ALU.mult),
                 waits=[("T", tb)], inc=self.rBc.s)
            P.op("vector",
                 lambda e, g4=g4, j=j: e.tensor_scalar(
                     T["kbbuf"].ap()[:, g4 * S + 512 * j:
                                     g4 * S + 512 * j + 512],
                     T["expkb"].ap(), 0.1, None, ALU.mult),
                 inc=self.rKb.s)
        self.rKb.release()
        self.rBc.release()

    def emit(self):
        self.prologue()
        P = self.P
        for bi in range(BC):
            for c in range(NCH):
                self.dma_g(lambda c=c, bi=bi: self.sch(c, 0, S),
                           lambda c=c, bi=bi: self.dram["xfeat"][bi, c])
            ldx = P.tot["LD"]
            for e in ("tensor", "vector", "scalar"):
                P.wait(e, [("LD", ldx)])
            P.barrier()
            ns = self.nsub
            if ns >= 1:
                self.attn_a(bi, final=(ns == 1))
            if ns >= 2:
                self.ffn(bi, final=(ns == 2))
            if ns >= 3:
                self.attn_b(bi, final=(ns == 3))
            if ns >= 4:
                self.ffn(bi, final=True)
            if ns in (0, 3):
                for c in range(NCH):
                    for j in range(NT5):
                        osl = 0
                        vv = P.op("vector",
                                  lambda e, c=c, j=j, osl=osl: e.tensor_copy(
                                      self.T["outsb"].ap()
                                      [:, osl * 512:osl * 512 + 512],
                                      self.sch(c, 512 * j, 512)),
                                  waits=[self.rOut[osl].wfree()], inc="V")
                        P.wait("sync", [("V", vv)])
                        self.dma_s(lambda c=c, j=j, bi=bi:
                                   self.yout[bi, c][:, 512 * j:512 * j + 512],
                                   lambda osl=osl: self.T["outsb"].ap()
                                   [:, osl * 512:osl * 512 + 512], inc="YD")
                        P.op("sync", lambda e: e.drain(),
                             waits=[("YD", P.tot["YD"])], inc=self.rOut[osl].s)
                        self.rOut[osl].release()
            P.barrier()
        P.wait("sync", [("YD", P.tot.get("YD", 0))])


def build_program(nsub=NSUB):
    nc = bass.Bass("TRN2", target_bir_lowering=False,
                   detect_race_conditions=False)
    with ExitStack() as stack:
        K = Kern(nc, stack, nsub)
        K.emit()
        block = stack.enter_context(nc.Block())
        K.P.finalize(block)
    return nc


def kernel(**inputs):
    w = _host_prepare(inputs)
    x = np.asarray(inputs["L0_attn_a_x"], np.float32)
    nc = build_program(NSUB)
    in_maps = []
    for core in range(8):
        xs = x[core * BC:(core + 1) * BC]
        xf = np.ascontiguousarray(xs.transpose(0, 2, 1)).reshape(BC, 8, 128, S)
        m = {"xfeat": xf}
        m.update(w)
        in_maps.append(m)
    res = run_bass_kernel_spmd(nc, in_maps, core_ids=list(range(8)))
    outs = []
    for core in range(8):
        yf = np.asarray(res.results[core]["yout"])
        outs.append(yf.reshape(BC, 1024, S).transpose(0, 2, 1))
    return np.concatenate(outs, 0).astype(np.float32)
